# version 01
# baseline (speedup 1.0000x reference)
import numpy as np
from contextlib import ExitStack

import concourse.bass as bass
import concourse.mybir as mybir
from concourse.bass_utils import run_bass_kernel_spmd

F32 = mybir.dt.float32
BF16 = mybir.dt.bfloat16
AF = mybir.ActivationFunctionType
ALU = mybir.AluOpType
AX = mybir.AxisListType

NCORE = 8
D = 2048
DCH = 16
TC = 512
FCH = 32
QCH = 24
NE = 8
EPS = 1e-5
SCALE_ATT = float(1.0 / np.sqrt(np.float32(128.0)))

ENGINES = ("tensor", "vector", "scalar", "gpsimd", "sync")
DMA_SEMS = frozenset(("g0", "g1", "s0", "s1", "cc"))
EPOCH = 2


class Tok:
    __slots__ = ("sem", "val", "eng", "fd")
    def __init__(self, sem, val, eng, fd=1 << 30):
        self.sem, self.val, self.eng, self.fd = sem, val, eng, fd


class EpochTok:
    __slots__ = ("queue", "epoch", "sem", "eng")
    def __init__(self, queue, epoch, sem, eng):
        self.queue, self.epoch, self.sem, self.eng = queue, epoch, sem, eng
    @property
    def val(self):
        raise RuntimeError("EpochTok.val is resolved at replay")


class Prog:
    def __init__(self, upto=99):
        self.ops = {e: [] for e in ENGINES}
        self.count = {}
        self.phase = 0
        self.upto = upto
        self.qepochs = {"gpsimd": [], "sync": []}

    def flush_epoch(self, queue):
        eps = self.qepochs[queue]
        if eps and eps[-1][0] < EPOCH:
            eps[-1][0] = eps[-1][0]
            eps[-1].append("closed")

    def dma_token(self, queue, eng):
        eps = self.qepochs[queue]
        if not eps or eps[-1][0] >= EPOCH or len(eps[-1]) > 1:
            eps.append([0])
        e = len(eps) - 1
        eps[-1][0] += 1
        sem = ("g" if queue == "gpsimd" else "s") + str(e % 2)
        gate = None
        if eps[-1][0] == 1 and e >= 2:
            gsem = ("g" if queue == "gpsimd" else "s") + str((e - 2) % 2)
            gate = EpochTok(queue, e - 2, gsem, "none")
        return EpochTok(queue, e, sem, eng), gate

    def resolve(self, tok):
        if isinstance(tok, EpochTok):
            eps = self.qepochs[tok.queue]
            n = sum(eps[i][0] for i in range(tok.epoch + 1)
                    if (("g" if tok.queue == "gpsimd" else "s") + str(i % 2)) == tok.sem)
            return tok.sem, 16 * n
        return tok.sem, tok.val

    def emit(self, eng, fn, waits=(), inc=None, inc_n=1, fd=1 << 30):
        if self.phase > self.upto:
            c = self.count.get(inc, 0) if inc is not None else 0
            return Tok(inc, c, eng, fd) if inc is not None else None
        wl = []
        for w in waits:
            force = False
            if isinstance(w, tuple):
                w, force = w
            if w is None:
                continue
            if isinstance(w, EpochTok) or w.sem in DMA_SEMS:
                force = True
            elif w.eng == eng and w.fd < 512:
                force = True
            if (not force) and w.eng == eng:
                continue
            wl.append(w)
        tok = None
        if inc is not None:
            c = self.count.get(inc, 0) + inc_n
            self.count[inc] = c
            tok = Tok(inc, c, eng, fd)
        self.ops[eng].append((wl, fn, inc, inc_n))
        return tok

    def emit_dma(self, queue, fn, waits=()):
        if self.phase > self.upto:
            eps = self.qepochs[queue]
            e = max(len(eps) - 1, 0)
            sem = ("g" if queue == "gpsimd" else "s") + str(e % 2)
            return EpochTok(queue, e, sem, queue)
        tok, gate = self.dma_token(queue, queue)
        wl = list(waits)
        if gate is not None:
            wl.append(gate)
        def fn2(g, fn=fn, sem=tok.sem):
            return fn(g, sem)
        self.ops[queue].append((
            [x for x in wl if x is not None], fn2, ("dma", tok.sem), 16))
        return tok

    def replay(self, block, sems):
        def mk(eng_name):
            def body(eng):
                seen = {}
                for wl, fn, inc, inc_n in self.ops[eng_name]:
                    for w in wl:
                        force = False
                        if isinstance(w, tuple):
                            w, force = w
                        if w is None:
                            continue
                        sem, val = self.resolve(w)
                        if seen.get(sem, -1) >= val:
                            continue
                        eng.wait_ge(sems[sem], val)
                        seen[sem] = val
                    inst = fn(eng)
                    if inc is not None:
                        if isinstance(inc, tuple):
                            inst.then_inc(sems[inc[1]], 16)
                        else:
                            inst.then_inc(sems[inc], inc_n)
            return body
        block.tensor(mk("tensor"))
        block.vector(mk("vector"))
        block.scalar(mk("scalar"))
        block.gpsimd(mk("gpsimd"))
        block.sync(mk("sync"))


def build_program(dbg=(), upto=99):
    dbg = set(dbg)
    nc = bass.Bass("TRN2", target_bir_lowering=False)

    ap = {}
    def din(name, shape):
        ap[name] = nc.dram_tensor(name, shape, F32, kind="ExternalInput").ap()

    din("x0", [2, D, TC])
    din("wmix", [3, DCH, 128, DCH, 128])
    din("wo", [DCH, 128, DCH, 128])
    din("wqkv", [QCH, 128, DCH, 128])
    din("wproj", [DCH, 128, DCH, 128])
    din("w1", [NE, FCH, 128, DCH, 128])
    din("w2", [NE, DCH, 128, FCH, 128])
    din("router", [DCH, 128, 8])
    din("gnw", [128, DCH])
    din("l1w", [128, DCH]); din("l1b", [128, DCH])
    din("l2w", [128, DCH]); din("l2b", [128, DCH])
    din("eye128", [128, 128])
    din("eye8", [8, 8, 128])
    din("pmask", [128, 8])

    yout = nc.dram_tensor("y", [2, D, TC], F32, kind="ExternalOutput").ap()
    dbg_outs = {}
    def mkdbg(name, shape, dtype=F32):
        if name in dbg:
            dbg_outs[name] = nc.dram_tensor("dbg_" + name, shape, dtype,
                                            kind="ExternalOutput").ap()

    mkdbg("a", [128, DCH * 1024])
    mkdbg("bsc", [128, DCH * 1024], BF16)
    mkdbg("H", [128, DCH * 1024], BF16)
    mkdbg("P", [128, DCH * 1024], BF16)
    mkdbg("ofinal", [128, DCH * 1024], BF16)
    mkdbg("x1", [128, DCH * 1024])
    mkdbg("ln1", [128, DCH * 1024], BF16)
    mkdbg("qkv", [128, DCH * 1024], BF16)
    mkdbg("attn", [128, DCH * 1024], BF16)
    mkdbg("ln2", [128, DCH * 1024], BF16)
    mkdbg("gates", [8, 1024])
    mkdbg("acc", [128, DCH * 1024])
    mkdbg("ends", [128, 64])
    mkdbg("mix0", [128, 3072], BF16)
    mkdbg("aggsnap", [128, 512])
    mkdbg("presnap", [128, 320])
    mkdbg("ccout", [NCORE * 128, 64])
    mkdbg("xb", [128, DCH * 1024], BF16)
    mkdbg("slab0", [128, 3 * 2048], BF16)
    mkdbg("agg", [128, NCORE * 64])
    mkdbg("carry", [128, 320])

    cc_in = nc.dram_tensor("cc_in", [128, 64], F32)
    cc_out = nc.dram_tensor("cc_out", [NCORE * 128, 64], F32)

    p = Prog(upto=upto)

    with ExitStack() as stk:
        def sbt(name, shape, dtype):
            return stk.enter_context(nc.sbuf_tensor(name, shape, dtype))

        bH = sbt("bH", [128, DCH * 1024], BF16)
        bP = sbt("bP", [128, DCH * 1024], BF16)
        actA = sbt("actA", [128, DCH * 1024], BF16)
        actB = sbt("actB", [128, DCH * 1024], BF16)
        actC = sbt("actC", [128, DCH * 1024], BF16)
        wsl = sbt("wsl", [128, 3 * 2048], BF16)
        a_sm = sbt("a_sm", [128, 1024], F32)
        s_sm = sbt("s_sm", [128, 1024], F32)
        c_sm = sbt("c_sm", [128, 1024], F32)
        d_sm = sbt("d_sm", [128, 1024], F32)
        e_sm = sbt("e_sm", [128, 1024], F32)
        ends = sbt("ends", [128, 64], F32)
        agg = sbt("agg", [128, NCORE * 64], F32)
        pre = sbt("pre", [128, 320], F32)
        routb = sbt("routb", [128, DCH * 8], BF16)
        gtm = sbt("gtm", [128, 80], F32)
        onesf = sbt("onesf", [128, 1], F32)
        onesb = sbt("onesb", [128, 1], BF16)
        ones1x = sbt("ones1x", [1, 128], F32)
        eyes = sbt("eyes", [128, 128], F32)
        eye8s = sbt("eye8s", [8, 8 * 128], F32)
        gnw_s = sbt("gnw_s", [128, DCH], F32)
        l1w_s = sbt("l1w_s", [128, DCH], F32)
        l1b_s = sbt("l1b_s", [128, DCH], F32)
        l2w_s = sbt("l2w_s", [128, DCH], F32)
        l2b_s = sbt("l2b_s", [128, DCH], F32)
        pmask_s = sbt("pmask_s", [128, 8], F32)
        mix0_s = sbt("mix0_s", [128, 3072], BF16)
        cEPS = sbt("cEPS", [128, 1], F32)

        B_h = [stk.enter_context(nc.psum_tensor(f"B{i}", [128, 512], F32))
               for i in range(8)]
        B = [h[:, :] for h in B_h]

        sems = {}
        for s_name in ("g0", "g1", "s0", "s1", "pe", "dv", "ac", "cc"):
            sems[s_name] = stk.enter_context(nc.semaphore(s_name))
        with nc.Block("semclear") as blk0:
            def _clear(v):
                for s in sems.values():
                    v.sem_clear(s)
            blk0.vector(_clear)
        block = stk.enter_context(nc.Block())

        def chunks(t):
            return [t[:, i * 1024:(i + 1) * 1024] for i in range(t.shape[1] // 1024)]

        bH_c, bP_c = chunks(bH), chunks(bP)
        actA_c, actB_c, actC_c = chunks(actA), chunks(actB), chunks(actC)
        residf = [t.bitcast(F32) for t in (bH_c + bP_c)]

        def resid_t(dc, tt):
            return residf[2 * dc + tt]

        xrC = [t.bitcast(F32) for t in actC_c]

        wslot = [wsl[:, i * 2048:(i + 1) * 2048] for i in range(3)]

        def slab_tile(slot, kc):
            return wslot[slot][:, kc * 128:(kc + 1) * 128]

        def TT(out, i0, i1, op, waits=(), inc="dv"):
            return p.emit("vector",
                          lambda v, out=out, i0=i0, i1=i1, op=op:
                          v.tensor_tensor(out=out, in0=i0, in1=i1, op=op),
                          waits, inc, fd=out.free_size())

        def TS(out, i0, s1v, s2v, op0, op1=None, waits=(), inc="dv"):
            def f(v, out=out, i0=i0, s1v=s1v, s2v=s2v, op0=op0, op1=op1):
                if op1 is None:
                    return v.tensor_scalar(out=out, in0=i0, scalar1=s1v,
                                           scalar2=None, op0=op0)
                return v.tensor_scalar(out=out, in0=i0, scalar1=s1v,
                                       scalar2=s2v, op0=op0, op1=op1)
            return p.emit("vector", f, waits, inc, fd=out.free_size())

        def STT(out, i0, sc, i1, op0, op1, waits=(), inc="dv"):
            return p.emit("vector",
                          lambda v, out=out, i0=i0, sc=sc, i1=i1, op0=op0, op1=op1:
                          v.scalar_tensor_tensor(out=out, in0=i0, scalar=sc,
                                                 in1=i1, op0=op0, op1=op1),
                          waits, inc, fd=out.free_size())

        def VCOPY(out, i0, waits=(), inc="dv"):
            return p.emit("vector",
                          lambda v, out=out, i0=i0: v.tensor_copy(out, i0),
                          waits, inc, fd=out.free_size())

        def VRED(out, i0, op, waits=(), inc="dv"):
            return p.emit("vector",
                          lambda v, out=out, i0=i0, op=op:
                          v.tensor_reduce(out=out, in_=i0, axis=AX.X, op=op),
                          waits, inc, fd=out.free_size())

        def SCAN(out, d0, d1, init, op1, waits=(), inc="dv"):
            return p.emit("vector",
                          lambda v, out=out, d0=d0, d1=d1, init=init, op1=op1:
                          v.tensor_tensor_scan(out=out, data0=d0, data1=d1,
                                               initial=init, op0=ALU.mult, op1=op1),
                          waits, inc, fd=out.free_size())

        def ACT(out, i0, func, scale=None, bias=None, waits=(), inc="ac"):
            def f(s, out=out, i0=i0, func=func, scale=scale, bias=bias):
                kw = {}
                if scale is not None:
                    kw["scale"] = scale
                if bias is not None:
                    kw["bias"] = bias
                return s.activation(out=out, in_=i0, func=func, **kw)
            return p.emit("scalar", f, waits, inc, fd=out.free_size())

        def MM(out, lhsT, rhs, start, stop, waits=(), inc="pe"):
            return p.emit("tensor",
                          lambda t, out=out, lhsT=lhsT, rhs=rhs, start=start, stop=stop:
                          t.matmul(out, lhsT=lhsT, rhs=rhs, start=start, stop=stop),
                          waits, inc)

        def TRANS(out, in_, waits=(), inc="pe"):
            return p.emit("tensor",
                          lambda t, out=out, in_=in_:
                          t.transpose(out, in_=in_, identity=eyes[:, :]),
                          waits, inc)

        def DMA(queue, out, in_, waits=(), inc=None):
            return p.emit_dma(queue,
                              lambda g, sem, out=out, in_=in_:
                              g.dma_start(out=out, in_=in_),
                              waits)

        def dump16(name, src_chunks, wtok):
            if name in dbg_outs:
                ph = p.phase
                p.phase = 0
                for i in range(DCH):
                    DMA("sync", dbg_outs[name][:, i * 1024:(i + 1) * 1024],
                        src_chunks[i], waits=[wtok])
                p.phase = ph

        gate = {}

        slab = {"idx": 0, "use": {}}

        def load_slab(src_block):
            i = slab["idx"]
            slot = i % 3
            waits = [slab["use"].get(i - 3)]
            tok = DMA("gpsimd", wslot[slot], src_block, waits=waits)
            slab["idx"] = i + 1
            return i, slot, tok

        def slab_done(i, tok):
            slab["use"][i] = tok

        def pack_src(base_ap):
            return base_ap.rearrange("p a b -> p (a b)")

        def mm_accum(bank, slot, rhs_of_kc, nk, waits_first):
            tok = None
            for kc in range(nk):
                w = waits_first if kc == 0 else ()
                tok = MM(bank, slab_tile(slot, kc), rhs_of_kc(kc),
                         start=(kc == 0), stop=(kc == nk - 1), waits=w)
            return tok

        t_ms = []
        t_ms.append(p.emit("vector", lambda v: v.memset(onesf[:, :], 1.0), (), "dv"))
        t_ms.append(p.emit("vector", lambda v: v.memset(onesb[:, :], 1.0), (), "dv"))
        t_ms.append(p.emit("vector", lambda v: v.memset(ones1x[:, :], 1.0), (), "dv"))
        t_ms.append(p.emit("vector", lambda v: v.memset(cEPS[:, :], EPS), (), "dv"))
        t_const = t_ms[-1]

        for wf_ in (AF.Sigmoid, AF.Ln, AF.Exp, AF.Gelu):
            ACT(s_sm[0:1, 0:1], cEPS[0:1, 0:1], wf_)

        t_par = {}
        for nm, dst in (("gnw", gnw_s), ("l1w", l1w_s), ("l1b", l1b_s),
                        ("l2w", l2w_s), ("l2b", l2b_s), ("eye128", eyes),
                        ("pmask", pmask_s)):
            t_par[nm] = DMA("sync", dst[:, :], ap[nm])
        t_par["eye8"] = DMA("sync", eye8s[:, :],
                            ap["eye8"].rearrange("a b c -> a (b c)"))
        t_rout = DMA("gpsimd", routb[:, :].rearrange("p (c e) -> p c e", c=DCH),
                     ap["router"].rearrange("c p e -> p c e"))

        for dc in range(DCH):
            src = ap["x0"][:, dc * 128:(dc + 1) * 128, :].rearrange("b p t -> p b t")
            dst = actA_c[dc].rearrange("p (b t) -> p b t", b=2)
            t_xb = DMA("gpsimd", dst, src)

        p.phase = 1
        t_scan_prev = None
        tG_prev = None
        t_ends_last = None
        t_fix_in = {}
        for oc in range(DCH):
            sl_f = load_slab(pack_src(ap["wmix"][0, oc]))
            sl_i = load_slab(pack_src(ap["wmix"][1, oc]))
            sl_g = load_slab(pack_src(ap["wmix"][2, oc]))
            mmt = {}
            for tt in range(2):
                mmt[tt] = mm_accum(
                    B[tt], sl_f[1],
                    lambda kc, tt=tt: actA_c[kc][:, tt * 512:(tt + 1) * 512], DCH,
                    [sl_f[2], gate.get(("B", tt))])
            slab_done(sl_f[0], mmt[1])
            if oc == 0 and "mix0" in dbg_outs:
                ph = p.phase; p.phase = 0
                ACT(mix0_s[:, 0:512], B[0], AF.Copy, waits=[mmt[0]])
                p.phase = ph
            ta0 = ACT(a_sm[:, 0:512], B[0], AF.Sigmoid,
                      waits=[mmt[0], t_scan_prev])
            ta1 = ACT(a_sm[:, 512:1024], B[1], AF.Sigmoid, waits=[mmt[1]])
            gate[("B", 0)], gate[("B", 1)] = ta0, ta1
            for tt in range(2):
                mmt[2 + tt] = mm_accum(
                    B[2 + tt], sl_i[1],
                    lambda kc, tt=tt: actA_c[kc][:, tt * 512:(tt + 1) * 512], DCH,
                    [sl_i[2], gate.get(("B", 2 + tt))])
            slab_done(sl_i[0], mmt[3])
            tsi0 = ACT(s_sm[:, 0:512], B[2], AF.Sigmoid, waits=[mmt[2], tG_prev])
            tsi1 = ACT(s_sm[:, 512:1024], B[3], AF.Sigmoid, waits=[mmt[3]])
            tb = {}
            for tt in range(2):
                sl_ = slice(tt * 512, (tt + 1) * 512)
                t1 = TT(c_sm[:, sl_], B[2 + tt], s_sm[:, sl_], ALU.mult,
                        waits=[tsi0 if tt == 0 else tsi1, t_scan_prev])
                t2 = TS(d_sm[:, sl_], a_sm[:, sl_], -1.0, 1.0, ALU.mult, ALU.add,
                        waits=[ta0 if tt == 0 else ta1])
                tb[tt] = TT(c_sm[:, sl_], c_sm[:, sl_], d_sm[:, sl_], ALU.mult)
            gate[("B", 2)], gate[("B", 3)] = tb[0], tb[1]
            for tt in range(2):
                mmt[4 + tt] = mm_accum(
                    B[4 + tt], sl_g[1],
                    lambda kc, tt=tt: actA_c[kc][:, tt * 512:(tt + 1) * 512], DCH,
                    [sl_g[2], gate.get(("B", 4 + tt))])
            slab_done(sl_g[0], mmt[5])
            tg0 = ACT(s_sm[:, 0:512], B[4], AF.Sigmoid, waits=[mmt[4], tb[0]])
            tg1 = ACT(s_sm[:, 512:1024], B[5], AF.Sigmoid, waits=[mmt[5], tb[1]])
            tG0 = TT(actB_c[oc][:, 0:512], B[4], s_sm[:, 0:512], ALU.mult,
                     waits=[tg0])
            tG1 = TT(actB_c[oc][:, 512:1024], B[5], s_sm[:, 512:1024], ALU.mult,
                     waits=[tg1])
            gate[("B", 4)], gate[("B", 5)] = tG0, tG1
            tG_prev = tG1
            for b in range(2):
                sl_ = slice(b * 512, (b + 1) * 512)
                th = SCAN(bH_c[oc][:, sl_], a_sm[:, sl_], c_sm[:, sl_], 0.0,
                          ALU.add, waits=[tb[b]])
                tp_ = SCAN(bP_c[oc][:, sl_], a_sm[:, sl_], a_sm[:, sl_], 1.0,
                           ALU.bypass)
                col_h, col_p = oc * 2 + b, 32 + oc * 2 + b
                VCOPY(ends[:, col_h:col_h + 1], bH_c[oc][:, sl_.stop - 1:sl_.stop])
                t_ends_last = VCOPY(ends[:, col_p:col_p + 1],
                                    bP_c[oc][:, sl_.stop - 1:sl_.stop])
                t_fix_in[(oc, b)] = (th, tp_)
            if oc == 0 and "mix0" in dbg_outs:
                ph = p.phase; p.phase = 0
                m0 = dbg_outs["mix0"]
                VCOPY(mix0_s[:, 1024:2048], a_sm[:, :])
                tbd = VCOPY(mix0_s[:, 2048:3072], bH_c[0])
                DMA("sync", m0, mix0_s[:, :], waits=[tbd])
                p.phase = ph
            t_scan_prev = t_ends_last

        if "xb" in dbg_outs:
            ph = p.phase; p.phase = 0
            dump16("xb", actA_c, t_ends_last)
            p.phase = ph
        if "slab0" in dbg_outs:
            ph = p.phase; p.phase = 0
            DMA("sync", dbg_outs["slab0"], wsl[:, :], waits=[t_ends_last])
            p.phase = ph
        p.phase = 2
        p.flush_epoch("gpsimd")
        t_stage = DMA("gpsimd", cc_in.ap(), ends[:, :], waits=[t_ends_last])
        p.flush_epoch("gpsimd")
        p.emit("gpsimd",
               lambda g: g.collective_compute(
                   "AllGather", ALU.bypass,
                   replica_groups=[list(range(NCORE))],
                   ins=[cc_in.ap().opt()],
                   outs=[cc_out.ap().opt()]),
               waits=[t_stage], inc="cc")
        t_agg = DMA("gpsimd", agg[:, :],
                    cc_out.ap().rearrange("(r p) c -> p r c", p=128),
                    waits=[Tok("cc", 1, "gpsimd")])
        aggv = agg[:, :].rearrange("p (r c) -> p r c", r=NCORE)
        if "aggsnap" in dbg_outs:
            ph = p.phase; p.phase = 0
            snapv = mix0_s[:, :].bitcast(F32)
            tsnap = VCOPY(snapv[:, 0:512], agg[:, :], waits=[t_agg])
            DMA("sync", dbg_outs["aggsnap"], snapv[:, 0:512], waits=[tsnap])
            p.phase = ph
        if "ccout" in dbg_outs:
            ph = p.phase; p.phase = 0
            DMA("sync", dbg_outs["ccout"], cc_out.ap(),
                waits=[Tok("cc", 1, "none")])
            p.phase = ph
        pre_r = [pre[:, r * 32:(r + 1) * 32] for r in range(NCORE)]
        carry = pre[:, 256:288]
        tpre = VCOPY(pre_r[0], aggv[:, 0, 0:32], waits=[t_agg])
        for r in range(1, NCORE):
            tmu_ = TT(pre[:, 288:320], pre_r[r - 1], aggv[:, r, 32:64], ALU.mult,
                      waits=[tpre])
            tpre = TT(pre_r[r], pre[:, 288:320], aggv[:, r, 0:32], ALU.add,
                      waits=[tmu_])
        if "aggsnap" in dbg_outs:
            ph = p.phase; p.phase = 0
            snapv2 = mix0_s[:, :].bitcast(F32)
            tsnap2 = VCOPY(snapv2[:, 512:832], pre[:, 0:320], waits=[(tpre, True)])
            DMA("sync", dbg_outs["presnap"], snapv2[:, 512:832], waits=[tsnap2])
            p.phase = ph
        tcar = p.emit("vector", lambda v: v.memset(carry, 0.0), (), "dv")
        for r in range(NCORE):
            tcar = STT(carry, pre_r[r], pmask_s[:, r:r + 1], carry, ALU.mult,
                       ALU.add, waits=[(tcar, True), t_par["pmask"]])
        t_fix = {}
        for oc in range(DCH):
            for b in range(2):
                sl_ = slice(b * 512, (b + 1) * 512)
                col = carry[:, oc * 2 + b:oc * 2 + b + 1]
                t_fix[(oc, b)] = STT(bH_c[oc][:, sl_], bP_c[oc][:, sl_], col,
                                     bH_c[oc][:, sl_], ALU.mult, ALU.add,
                                     waits=[(tcar, True)])
        t_fix_last = t_fix[(DCH - 1, 1)]


        dump16("H", bH_c, t_fix_last)
        dump16("P", bP_c, t_fix_last)
        if "ends" in dbg_outs:
            DMA("sync", dbg_outs["ends"], ends[:, :], waits=[t_ends_last])
        if "agg" in dbg_outs:
            DMA("sync", dbg_outs["agg"], agg[:, :], waits=[t_agg])
        if "carry" in dbg_outs:
            DMA("sync", dbg_outs["carry"], pre[:, :], waits=[(tcar, False)])

        p.phase = 3
        t_sq_mm = {0: None, 1: None}
        tq_prev = None
        for dc in range(DCH):
            tq = ACT(s_sm[:, :], bH_c[dc], AF.Square,
                     waits=[t_fix[(dc, 0)], t_fix[(dc, 1)],
                            t_sq_mm[1] if dc > 0 else None])
            for tt in range(2):
                bank = B[6][tt * 32:tt * 32 + 1, :]
                t_sq_mm[tt] = MM(bank, onesf[:, :],
                                 s_sm[:, tt * 512:(tt + 1) * 512],
                                 start=(dc == 0), stop=(dc == DCH - 1),
                                 waits=[tq, t_const, gate.get(("B6r", tt))]
                                 if dc == 0 else [tq])
            tq_prev = tq
        t_r1 = ACT(e_sm[0:1, 0:512], B[6][0:1, :], AF.Ln, scale=1.0 / D,
                   bias=cEPS[0:1, :], waits=[t_sq_mm[0], t_sq_mm[1]])
        t_r2 = ACT(e_sm[0:1, 512:1024], B[6][32:33, :], AF.Ln, scale=1.0 / D,
                   bias=cEPS[0:1, :])
        t_r3 = ACT(e_sm[0:1, 0:1024], e_sm[0:1, 0:1024], AF.Exp, scale=-0.5)
        gate[("B6r", 0)] = t_r1
        gate[("B6r", 1)] = t_r2
        t_b1 = MM(B[4], ones1x[:, :], e_sm[0:1, 0:512], True, True,
                  waits=[t_r3, gate[("B", 4)]])
        t_b2 = MM(B[5], ones1x[:, :], e_sm[0:1, 512:1024], True, True,
                  waits=[gate[("B", 5)]])
        t_e1 = VCOPY(e_sm[:, 0:512], B[4], waits=[t_b1, (t_r3, True)])
        t_e2 = VCOPY(e_sm[:, 512:1024], B[5], waits=[t_b2])
        gate[("B", 4)], gate[("B", 5)] = t_e1, t_e2
        t_of = {}
        for dc in range(DCH):
            t1 = STT(s_sm[:, :], bH_c[dc], gnw_s[:, dc:dc + 1], e_sm[:, :],
                     ALU.mult, ALU.mult,
                     waits=[t_e1, t_e2, t_par["gnw"], t_sq_mm[1]])
            t_of[dc] = TT(actA_c[dc], s_sm[:, :], actB_c[dc], ALU.mult)
        t_of_last = t_of[DCH - 1]
        dump16("ofinal", actA_c, t_of_last)

        p.phase = 4
        def ln_stats(dc, tt, src, sq, first, last, wf):
            bank_s = B[6][0:1, :] if tt == 0 else B[6][32:33, :]
            bank_q = B[6][64:65, :] if tt == 0 else B[7][0:1, :]
            ts_ = MM(bank_s, onesf[:, :], src, start=first, stop=last, waits=wf)
            tqq = MM(bank_q, onesf[:, :], sq, start=first, stop=last)
            return ts_, tqq

        def ln_rows(w_last):
            tmu0 = ACT(d_sm[0:1, 0:512], B[6][0:1, :], AF.Copy, scale=1.0 / D,
                       waits=list(w_last))
            tmu1 = ACT(d_sm[0:1, 512:1024], B[6][32:33, :], AF.Copy, scale=1.0 / D)
            tq0 = ACT(c_sm[0:1, 0:512], B[6][64:65, :], AF.Copy, scale=1.0 / D)
            tq1 = ACT(c_sm[0:1, 512:1024], B[7][0:1, :], AF.Copy, scale=1.0 / D)
            tv1 = TT(s_sm[0:1, :], d_sm[0:1, :], d_sm[0:1, :], ALU.mult,
                     waits=[tmu0, tmu1, tq0, tq1])
            tv2 = TT(s_sm[0:1, :], c_sm[0:1, :], s_sm[0:1, :], ALU.subtract)
            tl = ACT(e_sm[0:1, :], s_sm[0:1, :], AF.Ln, bias=cEPS[0:1, :],
                     waits=[tv2])
            te = ACT(e_sm[0:1, :], e_sm[0:1, :], AF.Exp, scale=-0.5)
            tb1 = MM(B[4], ones1x[:, :], d_sm[0:1, 0:512], True, True,
                     waits=[tmu1, te, gate.get(("B", 4))])
            tb2 = MM(B[5], ones1x[:, :], d_sm[0:1, 512:1024], True, True,
                     waits=[gate.get(("B", 5))])
            tc1 = VCOPY(d_sm[:, 0:512], B[4], waits=[tb1])
            tc2 = VCOPY(d_sm[:, 512:1024], B[5], waits=[tb2])
            tb3 = MM(B[4], ones1x[:, :], e_sm[0:1, 0:512], True, True,
                     waits=[te, tc1])
            tb4 = MM(B[5], ones1x[:, :], e_sm[0:1, 512:1024], True, True,
                     waits=[tc2])
            tc3 = VCOPY(e_sm[:, 0:512], B[4], waits=[tb3, (te, True)])
            tc4 = VCOPY(e_sm[:, 512:1024], B[5], waits=[tb4])
            gate[("B", 4)], gate[("B", 5)] = tc3, tc4
            gate[("B6r", 0)], gate[("B6r", 1)] = tmu0, tmu1
            gate[("B6r", 2)], gate[("B7r", 0)] = tq0, tq1
            return tc4

        t_xr = {}
        def xr_view(dc):
            s = dc % 2
            return actC[:, s * 2048:(s + 1) * 2048].bitcast(F32)

        def load_xr(dc, waits):
            src = ap["x0"][:, dc * 128:(dc + 1) * 128, :].rearrange("b p t -> p b t")
            dst = xr_view(dc).rearrange("p (b t) -> p b t", b=2)
            t_xr[dc] = DMA("sync", dst, src, waits=waits)

        load_xr(0, [t_par["eye8"]])
        load_xr(1, [])
        t_wo_ev = {}
        t_stats4_last = None
        for dc in range(DCH):
            sl_o = load_slab(pack_src(ap["wo"][dc]))
            mmt = {}
            for tt in range(2):
                mmt[tt] = mm_accum(
                    B[tt], sl_o[1],
                    lambda kc, tt=tt: actA_c[kc][:, tt * 512:(tt + 1) * 512], DCH,
                    [sl_o[2], t_of_last, gate.get(("B", tt))])
            slab_done(sl_o[0], mmt[1])
            xrv = xr_view(dc)
            for tt in range(2):
                sl_ = slice(tt * 512, (tt + 1) * 512)
                t1 = TT(resid_t(dc, tt), B[tt], xrv[:, sl_], ALU.add,
                        waits=[mmt[tt], t_xr[dc]])
                gate[("B", tt)] = t1
                t_wo_ev[(dc, tt)] = t1
                tqs = ACT(s_sm[:, sl_], resid_t(dc, tt), AF.Square,
                          waits=[t1, gate.get(("sq4", tt))])
                ts_, tqq = ln_stats(dc, tt, resid_t(dc, tt), s_sm[:, sl_],
                                    dc == 0, dc == DCH - 1,
                                    [t1, tqs] + ([gate.get(("B6r", tt)),
                                                  gate.get(("B6r", 2)),
                                                  gate.get(("B7r", 0))]
                                                 if dc == 0 else []))
                gate[("sq4", tt)] = tqq
                t_stats4_last = tqq
            if dc + 2 < DCH:
                load_xr(dc + 2, [t_wo_ev[(dc, 1)]])
        t_rows4 = ln_rows([t_stats4_last])
        t_x1 = {}
        for dc in range(DCH):
            for tt in range(2):
                sl_ = slice(tt * 512, (tt + 1) * 512)
                t1 = TT(a_sm[:, 0:512], resid_t(dc, tt), d_sm[:, sl_],
                        ALU.subtract, waits=[t_rows4] if dc == 0 and tt == 0 else ())
                t_x1[(dc, tt)] = TT(resid_t(dc, tt), a_sm[:, 0:512],
                                    e_sm[:, sl_], ALU.mult)
        t_x1_last = t_x1[(DCH - 1, 1)]
        if "x1" in dbg_outs:
            for dc in range(DCH):
                for tt in range(2):
                    DMA("sync", dbg_outs["x1"][:, dc * 1024 + tt * 512:
                                               dc * 1024 + (tt + 1) * 512],
                        resid_t(dc, tt), waits=[t_x1_last])

        p.phase = 5
        t_stats5_last = None
        for dc in range(DCH):
            for tt in range(2):
                sl_ = slice(tt * 512, (tt + 1) * 512)
                tqs = ACT(s_sm[:, sl_], resid_t(dc, tt), AF.Square,
                          waits=[t_x1[(dc, tt)], gate.get(("sq4", tt))])
                ts_, tqq = ln_stats(dc, tt, resid_t(dc, tt), s_sm[:, sl_],
                                    dc == 0, dc == DCH - 1,
                                    [t_x1[(dc, tt)], tqs] +
                                    ([gate.get(("B6r", 0)), gate.get(("B6r", 1)),
                                      gate.get(("B6r", 2)), gate.get(("B7r", 0))]
                                     if dc == 0 else []))
                gate[("sq4", tt)] = tqq
                t_stats5_last = tqq
        t_rows5 = ln_rows([t_stats5_last])
        t_ln1 = {}
        for dc in range(DCH):
            for tt in range(2):
                sl_ = slice(tt * 512, (tt + 1) * 512)
                TT(a_sm[:, 0:512], resid_t(dc, tt), d_sm[:, sl_], ALU.subtract,
                   waits=[t_rows5, tG_prev] if dc == 0 and tt == 0 else ())
                TT(a_sm[:, 0:512], a_sm[:, 0:512], e_sm[:, sl_], ALU.mult)
                t_ln1[(dc, tt)] = TS(actB_c[dc][:, sl_], a_sm[:, 0:512],
                                     l1w_s[:, dc:dc + 1], l1b_s[:, dc:dc + 1],
                                     ALU.mult, ALU.add,
                                     waits=[t_par["l1w"], t_par["l1b"]])
        t_ln1_last = t_ln1[(DCH - 1, 1)]
        dump16("ln1", actB_c, t_ln1_last)

        def q_tile(h):
            return actA[:, h * 512:(h + 1) * 512]
        def k_tile(g_):
            return actA[:, 8192 + g_ * 1024:8192 + (g_ + 1) * 1024]
        def v_tile(g_):
            return actA[:, 12288 + g_ * 1024:12288 + (g_ + 1) * 1024]

        t_qkv = {}
        for qc in range(QCH):
            g_, j = divmod(qc, 6)
            sl_q = load_slab(pack_src(ap["wqkv"][qc]))
            is_q = j <= 3
            for tt in range(2):
                if is_q and tt == 0:
                    continue
                bank = B[tt]
                mml = mm_accum(
                    bank, sl_q[1],
                    lambda kc, tt=tt: actB_c[kc][:, tt * 512:(tt + 1) * 512], DCH,
                    [sl_q[2], t_ln1_last, gate.get(("B", tt))])
                if is_q:
                    dst = q_tile(g_ * 4 + j)
                elif j == 4:
                    dst = k_tile(g_)[:, tt * 512:(tt + 1) * 512]
                else:
                    dst = v_tile(g_)[:, tt * 512:(tt + 1) * 512]
                tev = ACT(dst, bank, AF.Copy, waits=[mml, t_of_last])
                gate[("B", tt)] = tev
                t_qkv[(qc, tt)] = tev
                lastq = mml
            slab_done(sl_q[0], lastq)
        if "qkv" in dbg_outs:
            for i in range(DCH):
                DMA("sync", dbg_outs["qkv"][:, i * 1024:(i + 1) * 1024],
                    actA_c[i], waits=[t_qkv[(QCH - 1, 1)]])

        p.phase = 6
        kd_t = actC[:, 0:512]
        vd_t = actC[:, 512:1024]
        pr_t = actC[:, 1024:1536]
        t_attn_last = None
        t_pr_gate = None
        t_bc_gate = None
        for g_ in range(4):
            tkd = TT(kd_t, k_tile(g_)[:, 512:1024], k_tile(g_)[:, 0:512],
                     ALU.subtract,
                     waits=[t_qkv[(6 * g_ + 4, 0)], t_qkv[(6 * g_ + 4, 1)],
                            t_pr_gate])
            tvd = TT(vd_t, v_tile(g_)[:, 512:1024], v_tile(g_)[:, 0:512],
                     ALU.subtract,
                     waits=[t_qkv[(6 * g_ + 5, 0)], t_qkv[(6 * g_ + 5, 1)]])
            for j in range(4):
                h = g_ * 4 + j
                tpr = TT(pr_t, q_tile(h), kd_t, ALU.mult,
                         waits=[t_qkv[(6 * g_ + j, 1)], t_pr_gate])
                tdot = MM(B[6][0:1, :], onesb[:, :], pr_t, True, True,
                          waits=[tpr, t_const, gate.get("adot"),
                                 gate.get(("B6r", 0))])
                t_pr_gate = tdot
                trow = ACT(d_sm[0:1, 0:512], B[6][0:1, :], AF.Sigmoid,
                           scale=SCALE_ATT, waits=[tdot, t_bc_gate])
                gate["adot"] = trow
                tbc = MM(B[4], ones1x[:, :], d_sm[0:1, 0:512], True, True,
                         waits=[trow, gate.get(("B", 4))])
                t1 = TT(s_sm[:, 0:512], B[4], vd_t, ALU.mult, waits=[tbc])
                t_bc_gate = t1
                gate[("B", 4)] = t1
                t2 = TT(actB_c[h][:, 512:1024], v_tile(g_)[:, 0:512],
                        s_sm[:, 0:512], ALU.add)
                VCOPY(actB_c[h][:, 0:512], v_tile(g_)[:, 0:512])
                t_attn_last = t2
        t_attn_last = VCOPY(actB_c[15][:, 0:512], v_tile(3)[:, 0:512])
        dump16("attn", actB_c, t_attn_last)

        p.phase = 7
        t_stats7_last = None
        t_x2rb = {}
        t_acc = {}
        for dc in range(DCH):
            sl_p = load_slab(pack_src(ap["wproj"][dc]))
            mmt = {}
            for tt in range(2):
                mmt[tt] = mm_accum(
                    B[tt], sl_p[1],
                    lambda kc, tt=tt: actB_c[kc][:, tt * 512:(tt + 1) * 512], DCH,
                    [sl_p[2], t_attn_last, gate.get(("B", tt))])
            slab_done(sl_p[0], mmt[1])
            for tt in range(2):
                sl_ = slice(tt * 512, (tt + 1) * 512)
                tt1 = TT(c_sm[:, sl_], B[tt], resid_t(dc, tt), ALU.add,
                         waits=[mmt[tt], gate.get(("c7", tt))])
                gate[("B", tt)] = tt1
                tx = VCOPY(actA_c[dc][:, sl_], c_sm[:, sl_])
                t_x2rb[(dc, tt)] = tx
                ta = TT(resid_t(dc, tt), resid_t(dc, tt), c_sm[:, sl_], ALU.add)
                t_acc[(dc, tt)] = ta
                tqs = ACT(s_sm[:, sl_], c_sm[:, sl_], AF.Square,
                          waits=[tt1, gate.get(("sq4", tt))])
                ts_, tqq = ln_stats(dc, tt, c_sm[:, sl_], s_sm[:, sl_],
                                    dc == 0, dc == DCH - 1,
                                    [tt1, tqs, ta] +
                                    ([gate.get(("B6r", 0)), gate.get(("B6r", 1)),
                                      gate.get(("B6r", 2)), gate.get(("B7r", 0))]
                                     if dc == 0 else []))
                gate[("c7", tt)] = ts_
                gate[("sq4", tt)] = tqq
                t_stats7_last = tqq
        t_rows7 = ln_rows([t_stats7_last])
        t_ln2 = {}
        for dc in range(DCH):
            for tt in range(2):
                sl_ = slice(tt * 512, (tt + 1) * 512)
                TT(a_sm[:, 0:512], actA_c[dc][:, sl_], d_sm[:, sl_], ALU.subtract,
                   waits=[t_rows7, t_x2rb[(dc, tt)], t_attn_last]
                   if dc == 0 and tt == 0 else ())
                TT(a_sm[:, 0:512], a_sm[:, 0:512], e_sm[:, sl_], ALU.mult)
                t_ln2[(dc, tt)] = TS(actB_c[dc][:, sl_], a_sm[:, 0:512],
                                     l2w_s[:, dc:dc + 1], l2b_s[:, dc:dc + 1],
                                     ALU.mult, ALU.add,
                                     waits=[t_par["l2w"], t_par["l2b"]])
        t_ln2_last = t_ln2[(DCH - 1, 1)]
        dump16("ln2", actB_c, t_ln2_last)
        if "acc" in dbg_outs:
            for dc in range(DCH):
                for tt in range(2):
                    DMA("sync", dbg_outs["acc"][:, dc * 1024 + tt * 512:
                                                dc * 1024 + (tt + 1) * 512],
                        resid_t(dc, tt), waits=[t_acc[(DCH - 1, 1)]])

        p.phase = 8
        import os
        p8_level = int(os.environ.get("P8_LEVEL", "3"))
        gfm = d_sm
        t_log = {}
        for j in range(8):
            tt, lo = j // 4, (j % 4) * 128
            tok = None
            for dc in range(DCH):
                tok = MM(B[7][:, 8 * j:8 * j + 8],
                         actB_c[dc][:, tt * 512 + lo:tt * 512 + lo + 128],
                         routb[:, dc * 8:(dc + 1) * 8],
                         start=(dc == 0), stop=(dc == DCH - 1),
                         waits=[t_ln2_last, t_rout, gate.get(("B7r", 0))]
                         if j == 0 and dc == 0 else
                         ([t_ln2_last, t_rout] if dc == 0 else ()))
            t_log[j] = tok
        t_gate_last = None
        t_tr_gate = None
        for j in range(8 if p8_level >= 2 else 0):
            m1 = gtm[:, 0:1]; m2 = gtm[:, 1:2]; den = gtm[:, 2:3]
            r8 = gtm[:, 8:16]; e8 = gtm[:, 16:24]; msk = gtm[:, 24:32]
            gt8 = gtm[:, 32:40]; lg = gtm[:, 48:56]
            nops = int(os.environ.get("P8_OPS", "99"))
            tlg = VCOPY(lg, B[7][:, 8 * j:8 * j + 8], waits=[t_log[7], t_rows7])
            tm1 = VRED(m1, lg, ALU.max, waits=[tlg]) if nops >= 2 else tlg
            ti = TS(r8, lg, m1, None, ALU.is_ge, waits=[tm1]) if nops >= 3 else tm1
            tl = STT(r8, r8, -1e9, lg, ALU.mult, ALU.add, waits=[ti]) if nops >= 4 else ti
            tm2 = VRED(m2, r8, ALU.max, waits=[tl]) if nops >= 5 else tl
            te = TS(e8, lg, m1, None, ALU.subtract, waits=[tm1]) if nops >= 6 else tm2
            tex = ACT(e8, e8, AF.Exp, waits=[te, gate.get("gact")]) if nops >= 7 else te
            tmsk = TS(msk, lg, m2, None, ALU.is_ge, waits=[tm2]) if nops >= 8 else tex
            td1 = TT(den, m2, m1, ALU.subtract, waits=[tm2, tm1]) if nops >= 9 else tmsk
            td2 = ACT(den, den, AF.Exp, waits=[td1]) if nops >= 10 else td1
            gate["gact"] = td2
            td3 = TS(den, den, 1.0, None, ALU.add, waits=[td2]) if nops >= 11 else td2
            td4 = p.emit("vector", lambda v, den=den: v.reciprocal(out=den, in_=den),
                         [td3], "dv", fd=1) if nops >= 12 else td3
            tg1 = TT(e8, e8, msk, ALU.mult, waits=[tex, tmsk]) if nops >= 13 else td4
            tg2 = TS(gt8, e8, den, None, ALU.mult,
                     waits=[td4, tg1, t_tr_gate]) if nops >= 14 else tg1
            if p8_level >= 3:
                ttr = TRANS(B[5][0:8, 0:128], gt8,
                            waits=[tg2, t_par["eye128"], gate.get(("B", 5))])
                t_tr_gate = ttr
                tcp = VCOPY(gfm[0:8, j * 128:(j + 1) * 128], B[5][0:8, 0:128],
                            waits=[ttr])
                gate[("B", 5)] = tcp
                t_gate_last = tcp
            else:
                t_gate_last = tg2
        if "gates" in dbg_outs:
            DMA("sync", dbg_outs["gates"], gfm[0:8, :], waits=[t_gate_last])

        p.phase = 9
        def h1_tile(fc):
            return actA_c[fc] if fc < DCH else actC_c[fc - DCH]

        t_h1_used = None
        t_gb_prev = None
        for e in range(NE):
            tgb = {}
            for tt in range(2):
                tbk = MM(B[4 + tt], eye8s[0:8, e * 128:(e + 1) * 128],
                         gfm[0:8, tt * 512:(tt + 1) * 512], True, True,
                         waits=[t_gate_last, t_par["eye8"], gate.get(("B", 4 + tt))])
                tcp = VCOPY(e_sm[:, tt * 512:(tt + 1) * 512], B[4 + tt],
                            waits=[tbk, t_gb_prev])
                gate[("B", 4 + tt)] = tcp
                tgb[tt] = tcp
            t_gb_prev = tgb[1]
            tA_last = {}
            for fc in range(FCH):
                sl_1 = load_slab(pack_src(ap["w1"][e, fc]))
                mml = {}
                for tt in range(2):
                    bidx = 2 * (fc % 2) + tt
                    mml[tt] = mm_accum(
                        B[bidx], sl_1[1],
                        lambda kc, tt=tt: actB_c[kc][:, tt * 512:(tt + 1) * 512],
                        DCH, [sl_1[2], t_ln2_last, gate.get(("B", bidx))])
                    tev = ACT(h1_tile(fc)[:, tt * 512:(tt + 1) * 512], B[bidx],
                              AF.Gelu,
                              waits=[mml[tt], t_h1_used,
                                     (t_qkv[(QCH - 1, 1)] if e == 0 else None),
                                     (t_x2rb[(DCH - 1, 1)] if e == 0 else None)])
                    gate[("B", bidx)] = tev
                    tA_last[tt] = tev
                slab_done(sl_1[0], mml[1])
            tB_mm_last = None
            for dc in range(DCH):
                halves = []
                for hf in range(2):
                    srcv = ap["w2"][e, dc][:, hf * 16:(hf + 1) * 16, :]
                    halves.append(load_slab(pack_src(srcv)))
                mmt = {}
                for tt in range(2):
                    bidx = 2 * (dc % 2) + tt
                    tok = None
                    for hf in range(2):
                        sl_2 = halves[hf]
                        for kcl in range(DCH):
                            kc = hf * DCH + kcl
                            w = [sl_2[2], tA_last[0], tA_last[1],
                                 gate.get(("B", bidx))] if kcl == 0 else ()
                            tok = MM(B[bidx], slab_tile(sl_2[1], kcl),
                                     h1_tile(kc)[:, tt * 512:(tt + 1) * 512],
                                     start=(hf == 0 and kcl == 0),
                                     stop=(hf == 1 and kcl == DCH - 1), waits=w)
                    mmt[tt] = tok
                slab_done(halves[0][0], mmt[1])
                slab_done(halves[1][0], mmt[1])
                for tt in range(2):
                    sl_ = slice(tt * 512, (tt + 1) * 512)
                    t1 = TT(c_sm[:, sl_], B[2 * (dc % 2) + tt], e_sm[:, sl_],
                            ALU.mult,
                            waits=[mmt[tt], tgb[tt],
                                   (t_acc[(dc, tt)] if e == 0 else None),
                                   gate.get(("c9", tt))])
                    gate[("B", 2 * (dc % 2) + tt)] = t1
                    t2 = TT(resid_t(dc, tt), resid_t(dc, tt), c_sm[:, sl_],
                            ALU.add)
                    gate[("c9", tt)] = t2
                tB_mm_last = mmt[1]
            t_h1_used = tB_mm_last
        t_moe_last = gate[("c9", 1)]

        p.phase = 10
        t_stats10_last = None
        for dc in range(DCH):
            for tt in range(2):
                sl_ = slice(tt * 512, (tt + 1) * 512)
                tqs = ACT(s_sm[:, sl_], resid_t(dc, tt), AF.Square,
                          waits=[t_moe_last, gate.get(("sq4", tt))])
                ts_, tqq = ln_stats(dc, tt, resid_t(dc, tt), s_sm[:, sl_],
                                    dc == 0, dc == DCH - 1,
                                    [t_moe_last, tqs] +
                                    ([gate.get(("B6r", 0)), gate.get(("B6r", 1)),
                                      gate.get(("B6r", 2)), gate.get(("B7r", 0)),
                                      gate.get("adot")] if dc == 0 else []))
                gate[("sq4", tt)] = tqq
                t_stats10_last = tqq
        t_rows10 = ln_rows([t_stats10_last])
        t_st_gate = {}
        for dc in range(DCH):
            stage = s_sm if dc % 2 == 0 else c_sm
            for tt in range(2):
                sl_ = slice(tt * 512, (tt + 1) * 512)
                t1 = TT(a_sm[:, 0:512], resid_t(dc, tt), d_sm[:, sl_],
                        ALU.subtract,
                        waits=[t_rows10] if dc == 0 and tt == 0 else ())
                t2 = TT(stage[:, sl_], a_sm[:, 0:512], e_sm[:, sl_], ALU.mult,
                        waits=[t_st_gate.get((dc % 2, tt))])
                dst = yout[tt, dc * 128:(dc + 1) * 128, :]
                t_st_gate[(dc % 2, tt)] = DMA("sync", dst, stage[:, sl_],
                                              waits=[t2])
        p.phase = 0
        fin = []
        for q, pref in (("sync", "s"), ("gpsimd", "g")):
            eps = p.qepochs[q]
            for par in (0, 1):
                n = sum(eps[i][0] for i in range(len(eps)) if i % 2 == par)
                if n:
                    fin.append((Tok(pref + str(par), 16 * n, "none"), True))
        p.emit("sync", lambda sy: sy.nop(), waits=fin)

        p.replay(block, sems)

    return nc, dbg_outs



def _pack_lhsT(w):
    K, M = w.shape
    return np.ascontiguousarray(
        w.reshape(K // 128, 128, M // 128, 128).transpose(2, 1, 0, 3))


def _prep_inputs(inputs):
    f32 = np.float32
    x = np.asarray(inputs["L0_attn_hidden_states"], f32)
    xT = np.ascontiguousarray(x.transpose(0, 2, 1))

    wmix = np.stack([_pack_lhsT(np.asarray(inputs["L0_attn_Wf"], f32)),
                     _pack_lhsT(np.asarray(inputs["L0_attn_Wi"], f32)),
                     _pack_lhsT(np.asarray(inputs["L0_attn_Wg"], f32))])
    wo = _pack_lhsT(np.asarray(inputs["L0_attn_Wo"], f32))
    wqkv = _pack_lhsT(np.ascontiguousarray(
        np.asarray(inputs["L0_ffn_qkv_weight"], f32).T))
    wproj = _pack_lhsT(np.ascontiguousarray(
        np.asarray(inputs["L0_ffn_proj_weight"], f32).T))
    w1 = np.stack([_pack_lhsT(np.asarray(inputs["L0_ffn_moe_w1"], f32)[e])
                   for e in range(NE)])
    w2 = np.stack([_pack_lhsT(np.asarray(inputs["L0_ffn_moe_w2"], f32)[e])
                   for e in range(NE)])
    router = np.ascontiguousarray(
        np.asarray(inputs["L0_ffn_router_weight"], f32).reshape(DCH, 128, 8))

    def col(v):
        return np.ascontiguousarray(np.asarray(v, f32).reshape(DCH, 128).T)

    shared = {
        "wmix": wmix, "wo": wo, "wqkv": wqkv, "wproj": wproj,
        "w1": w1, "w2": w2, "router": router,
        "gnw": col(inputs["L0_attn_g_norm_weight"]),
        "l1w": col(inputs["L0_ffn_ln1_weight"]),
        "l1b": col(inputs["L0_ffn_ln1_bias"]),
        "l2w": col(inputs["L0_ffn_ln2_weight"]),
        "l2b": col(inputs["L0_ffn_ln2_bias"]),
        "eye128": np.eye(128, dtype=f32),
    }
    eye8 = np.zeros((8, 8, 128), f32)
    for e in range(8):
        eye8[e, e, :] = 1.0
    shared["eye8"] = eye8

    in_maps = []
    for c in range(NCORE):
        pm = np.zeros((128, 8), f32)
        if c > 0:
            pm[:, c - 1] = 1.0
        m = dict(shared)
        m["x0"] = np.ascontiguousarray(xT[:, :, c * TC:(c + 1) * TC])
        m["pmask"] = pm
        in_maps.append(m)
    return in_maps


def kernel(**inputs):
    in_maps = _prep_inputs(inputs)
    nc, _ = build_program()
    res = run_bass_kernel_spmd(nc, in_maps, core_ids=list(range(NCORE)))
    out = np.concatenate([res.results[c]["y"] for c in range(NCORE)], axis=2)
    return np.ascontiguousarray(out.transpose(0, 2, 1)).astype(np.float32)


# version 02
# speedup vs baseline: 1.0598x; 1.0598x over previous
import numpy as np
from contextlib import ExitStack

import concourse.bass as bass
import concourse.mybir as mybir
from concourse.bass_utils import run_bass_kernel_spmd

F32 = mybir.dt.float32
BF16 = mybir.dt.bfloat16
AF = mybir.ActivationFunctionType
ALU = mybir.AluOpType
AX = mybir.AxisListType

NCORE = 8
D = 2048
DCH = 16
TC = 512
FCH = 32
QCH = 24
NE = 8
EPS = 1e-5
SCALE_ATT = float(1.0 / np.sqrt(np.float32(128.0)))

ENGINES = ("tensor", "vector", "scalar", "gpsimd", "sync")
DMA_SEMS = frozenset(("g0", "g1", "s0", "s1", "cc"))
EPOCH = 2


class Tok:
    __slots__ = ("sem", "val", "eng", "fd")
    def __init__(self, sem, val, eng, fd=1 << 30):
        self.sem, self.val, self.eng, self.fd = sem, val, eng, fd


class EpochTok:
    __slots__ = ("queue", "epoch", "sem", "eng")
    def __init__(self, queue, epoch, sem, eng):
        self.queue, self.epoch, self.sem, self.eng = queue, epoch, sem, eng
    @property
    def val(self):
        raise RuntimeError("EpochTok.val is resolved at replay")


class Prog:
    def __init__(self, upto=99):
        self.ops = {e: [] for e in ENGINES}
        self.count = {}
        self.phase = 0
        self.upto = upto
        self.qepochs = {"gpsimd": [], "sync": []}

    def flush_epoch(self, queue):
        eps = self.qepochs[queue]
        if eps and eps[-1][0] < EPOCH:
            eps[-1][0] = eps[-1][0]
            eps[-1].append("closed")

    def dma_token(self, queue, eng):
        eps = self.qepochs[queue]
        if not eps or eps[-1][0] >= EPOCH or len(eps[-1]) > 1:
            eps.append([0])
        e = len(eps) - 1
        eps[-1][0] += 1
        sem = ("g" if queue == "gpsimd" else "s") + str(e % 2)
        gate = None
        if eps[-1][0] == 1 and e >= 2:
            gsem = ("g" if queue == "gpsimd" else "s") + str((e - 2) % 2)
            gate = EpochTok(queue, e - 2, gsem, "none")
        return EpochTok(queue, e, sem, eng), gate

    def resolve(self, tok):
        if isinstance(tok, EpochTok):
            eps = self.qepochs[tok.queue]
            n = sum(eps[i][0] for i in range(tok.epoch + 1)
                    if (("g" if tok.queue == "gpsimd" else "s") + str(i % 2)) == tok.sem)
            return tok.sem, 16 * n
        return tok.sem, tok.val

    def emit(self, eng, fn, waits=(), inc=None, inc_n=1, fd=1 << 30):
        if self.phase > self.upto:
            c = self.count.get(inc, 0) if inc is not None else 0
            return Tok(inc, c, eng, fd) if inc is not None else None
        wl = []
        for w in waits:
            force = False
            if isinstance(w, tuple):
                w, force = w
            if w is None:
                continue
            if isinstance(w, EpochTok) or w.sem in DMA_SEMS:
                force = True
            elif w.eng == eng and w.fd < 512:
                force = True
            if (not force) and w.eng == eng:
                continue
            wl.append(w)
        tok = None
        if inc is not None:
            c = self.count.get(inc, 0) + inc_n
            self.count[inc] = c
            tok = Tok(inc, c, eng, fd)
        self.ops[eng].append((wl, fn, inc, inc_n))
        return tok

    def emit_dma(self, queue, fn, waits=()):
        if self.phase > self.upto:
            eps = self.qepochs[queue]
            e = max(len(eps) - 1, 0)
            sem = ("g" if queue == "gpsimd" else "s") + str(e % 2)
            return EpochTok(queue, e, sem, queue)
        tok, gate = self.dma_token(queue, queue)
        wl = list(waits)
        if gate is not None:
            wl.append(gate)
        def fn2(g, fn=fn, sem=tok.sem):
            return fn(g, sem)
        self.ops[queue].append((
            [x for x in wl if x is not None], fn2, ("dma", tok.sem), 16))
        return tok

    def replay(self, block, sems):
        def mk(eng_name):
            def body(eng):
                seen = {}
                for wl, fn, inc, inc_n in self.ops[eng_name]:
                    for w in wl:
                        force = False
                        if isinstance(w, tuple):
                            w, force = w
                        if w is None:
                            continue
                        sem, val = self.resolve(w)
                        if seen.get(sem, -1) >= val:
                            continue
                        eng.wait_ge(sems[sem], val)
                        seen[sem] = val
                    inst = fn(eng)
                    if inc is not None:
                        if isinstance(inc, tuple):
                            inst.then_inc(sems[inc[1]], 16)
                        else:
                            inst.then_inc(sems[inc], inc_n)
            return body
        block.tensor(mk("tensor"))
        block.vector(mk("vector"))
        block.scalar(mk("scalar"))
        block.gpsimd(mk("gpsimd"))
        block.sync(mk("sync"))


def build_program(dbg=(), upto=99):
    dbg = set(dbg)
    nc = bass.Bass("TRN2", target_bir_lowering=False)

    ap = {}
    def din(name, shape):
        ap[name] = nc.dram_tensor(name, shape, F32, kind="ExternalInput").ap()

    din("x0", [2, D, TC])
    din("wmix", [3, DCH, 128, DCH, 128])
    din("wo", [DCH, 128, DCH, 128])
    din("wqkv", [QCH, 128, DCH, 128])
    din("wproj", [DCH, 128, DCH, 128])
    din("w1", [NE, FCH, 128, DCH, 128])
    din("w2", [NE, DCH, 128, FCH, 128])
    din("router", [DCH, 128, 8])
    din("gnw", [128, DCH])
    din("l1w", [128, DCH]); din("l1b", [128, DCH])
    din("l2w", [128, DCH]); din("l2b", [128, DCH])
    din("eye128", [128, 128])
    din("eye8", [8, 8, 128])
    din("pmask", [128, 8])

    yout = nc.dram_tensor("y", [2, D, TC], F32, kind="ExternalOutput").ap()
    dbg_outs = {}
    def mkdbg(name, shape, dtype=F32):
        if name in dbg:
            dbg_outs[name] = nc.dram_tensor("dbg_" + name, shape, dtype,
                                            kind="ExternalOutput").ap()

    mkdbg("a", [128, DCH * 1024])
    mkdbg("bsc", [128, DCH * 1024], BF16)
    mkdbg("H", [128, DCH * 1024], BF16)
    mkdbg("P", [128, DCH * 1024], BF16)
    mkdbg("ofinal", [128, DCH * 1024], BF16)
    mkdbg("x1", [128, DCH * 1024])
    mkdbg("ln1", [128, DCH * 1024], BF16)
    mkdbg("qkv", [128, DCH * 1024], BF16)
    mkdbg("attn", [128, DCH * 1024], BF16)
    mkdbg("ln2", [128, DCH * 1024], BF16)
    mkdbg("gates", [8, 1024])
    mkdbg("acc", [128, DCH * 1024])
    mkdbg("ends", [128, 64])
    mkdbg("mix0", [128, 3072], BF16)
    mkdbg("aggsnap", [128, 512])
    mkdbg("presnap", [128, 320])
    mkdbg("ccout", [NCORE * 128, 64])
    mkdbg("xb", [128, DCH * 1024], BF16)
    mkdbg("slab0", [128, 3 * 2048], BF16)
    mkdbg("agg", [128, NCORE * 64])
    mkdbg("carry", [128, 320])

    cc_in = nc.dram_tensor("cc_in", [128, 64], F32)
    cc_out = nc.dram_tensor("cc_out", [NCORE * 128, 64], F32)

    p = Prog(upto=upto)

    with ExitStack() as stk:
        def sbt(name, shape, dtype):
            return stk.enter_context(nc.sbuf_tensor(name, shape, dtype))

        bH = sbt("bH", [128, DCH * 1024], BF16)
        bP = sbt("bP", [128, DCH * 1024], BF16)
        actA = sbt("actA", [128, DCH * 1024], BF16)
        actB = sbt("actB", [128, DCH * 1024], BF16)
        actC = sbt("actC", [128, DCH * 1024], BF16)
        wsl = sbt("wsl", [128, 4 * 2048], BF16)
        a_sm = sbt("a_sm", [128, 1024], F32)
        s_sm = sbt("s_sm", [128, 1024], F32)
        c_sm = sbt("c_sm", [128, 1024], F32)
        d_sm = sbt("d_sm", [128, 1024], F32)
        e_sm = sbt("e_sm", [128, 1024], F32)
        ends = sbt("ends", [128, 64], F32)
        agg = sbt("agg", [128, NCORE * 64], F32)
        pre = sbt("pre", [128, 320], F32)
        routb = sbt("routb", [128, DCH * 8], BF16)
        gtm = sbt("gtm", [128, 144], F32)
        onesf = sbt("onesf", [128, 1], F32)
        onesb = sbt("onesb", [128, 1], BF16)
        ones1x = sbt("ones1x", [1, 128], F32)
        eyes = sbt("eyes", [128, 128], F32)
        eye8s = sbt("eye8s", [8, 8 * 128], F32)
        gnw_s = sbt("gnw_s", [128, DCH], F32)
        l1w_s = sbt("l1w_s", [128, DCH], F32)
        l1b_s = sbt("l1b_s", [128, DCH], F32)
        l2w_s = sbt("l2w_s", [128, DCH], F32)
        l2b_s = sbt("l2b_s", [128, DCH], F32)
        pmask_s = sbt("pmask_s", [128, 8], F32)
        mix0_s = sbt("mix0_s", [128, 3072], BF16) if ("mix0" in dbg or "aggsnap" in dbg) else None
        cEPS = sbt("cEPS", [128, 1], F32)

        B_h = [stk.enter_context(nc.psum_tensor(f"B{i}", [128, 512], F32))
               for i in range(8)]
        B = [h[:, :] for h in B_h]

        sems = {}
        for s_name in ("g0", "g1", "s0", "s1", "pe", "dv", "ac", "cc"):
            sems[s_name] = stk.enter_context(nc.semaphore(s_name))
        with nc.Block("semclear") as blk0:
            def _clear(v):
                for s in sems.values():
                    v.sem_clear(s)
            blk0.vector(_clear)
        block = stk.enter_context(nc.Block())

        def chunks(t):
            return [t[:, i * 1024:(i + 1) * 1024] for i in range(t.shape[1] // 1024)]

        bH_c, bP_c = chunks(bH), chunks(bP)
        actA_c, actB_c, actC_c = chunks(actA), chunks(actB), chunks(actC)
        residf = [t.bitcast(F32) for t in (bH_c + bP_c)]

        def resid_t(dc, tt):
            return residf[2 * dc + tt]

        xrC = [t.bitcast(F32) for t in actC_c]

        wslot = [wsl[:, i * 2048:(i + 1) * 2048] for i in range(4)]

        def slab_tile(slot, kc):
            return wslot[slot][:, kc * 128:(kc + 1) * 128]

        def TT(out, i0, i1, op, waits=(), inc="dv"):
            return p.emit("vector",
                          lambda v, out=out, i0=i0, i1=i1, op=op:
                          v.tensor_tensor(out=out, in0=i0, in1=i1, op=op),
                          waits, inc, fd=out.free_size())

        def TS(out, i0, s1v, s2v, op0, op1=None, waits=(), inc="dv"):
            def f(v, out=out, i0=i0, s1v=s1v, s2v=s2v, op0=op0, op1=op1):
                if op1 is None:
                    return v.tensor_scalar(out=out, in0=i0, scalar1=s1v,
                                           scalar2=None, op0=op0)
                return v.tensor_scalar(out=out, in0=i0, scalar1=s1v,
                                       scalar2=s2v, op0=op0, op1=op1)
            return p.emit("vector", f, waits, inc, fd=out.free_size())

        def STT(out, i0, sc, i1, op0, op1, waits=(), inc="dv"):
            return p.emit("vector",
                          lambda v, out=out, i0=i0, sc=sc, i1=i1, op0=op0, op1=op1:
                          v.scalar_tensor_tensor(out=out, in0=i0, scalar=sc,
                                                 in1=i1, op0=op0, op1=op1),
                          waits, inc, fd=out.free_size())

        def VCOPY(out, i0, waits=(), inc="dv"):
            return p.emit("vector",
                          lambda v, out=out, i0=i0: v.tensor_copy(out, i0),
                          waits, inc, fd=out.free_size())

        def VRED(out, i0, op, waits=(), inc="dv"):
            return p.emit("vector",
                          lambda v, out=out, i0=i0, op=op:
                          v.tensor_reduce(out=out, in_=i0, axis=AX.X, op=op),
                          waits, inc, fd=out.free_size())

        def SCAN(out, d0, d1, init, op1, waits=(), inc="dv"):
            return p.emit("vector",
                          lambda v, out=out, d0=d0, d1=d1, init=init, op1=op1:
                          v.tensor_tensor_scan(out=out, data0=d0, data1=d1,
                                               initial=init, op0=ALU.mult, op1=op1),
                          waits, inc, fd=out.free_size())

        def ACT(out, i0, func, scale=None, bias=None, waits=(), inc="ac"):
            def f(s, out=out, i0=i0, func=func, scale=scale, bias=bias):
                kw = {}
                if scale is not None:
                    kw["scale"] = scale
                if bias is not None:
                    kw["bias"] = bias
                return s.activation(out=out, in_=i0, func=func, **kw)
            return p.emit("scalar", f, waits, inc, fd=out.free_size())

        def MM(out, lhsT, rhs, start, stop, waits=(), inc="pe"):
            return p.emit("tensor",
                          lambda t, out=out, lhsT=lhsT, rhs=rhs, start=start, stop=stop:
                          t.matmul(out, lhsT=lhsT, rhs=rhs, start=start, stop=stop),
                          waits, inc)

        def TRANS(out, in_, waits=(), inc="pe"):
            return p.emit("tensor",
                          lambda t, out=out, in_=in_:
                          t.transpose(out, in_=in_, identity=eyes[:, :]),
                          waits, inc)

        def DMA(queue, out, in_, waits=(), inc=None):
            return p.emit_dma(queue,
                              lambda g, sem, out=out, in_=in_:
                              g.dma_start(out=out, in_=in_),
                              waits)

        def dump16(name, src_chunks, wtok):
            if name in dbg_outs:
                ph = p.phase
                p.phase = 0
                for i in range(DCH):
                    DMA("sync", dbg_outs[name][:, i * 1024:(i + 1) * 1024],
                        src_chunks[i], waits=[wtok])
                p.phase = ph

        gate = {}

        slab = {"idx": 0, "use": {}}

        def load_slab(src_block):
            i = slab["idx"]
            slot = i % 4
            waits = [slab["use"].get(i - 4)]
            tok = DMA("gpsimd", wslot[slot], src_block, waits=waits)
            slab["idx"] = i + 1
            return i, slot, tok

        def slab_done(i, tok):
            slab["use"][i] = tok

        def pack_src(base_ap):
            return base_ap.rearrange("p a b -> p (a b)")

        def mm_accum(bank, slot, rhs_of_kc, nk, waits_first):
            tok = None
            for kc in range(nk):
                w = waits_first if kc == 0 else ()
                tok = MM(bank, slab_tile(slot, kc), rhs_of_kc(kc),
                         start=(kc == 0), stop=(kc == nk - 1), waits=w)
            return tok

        t_ms = []
        t_ms.append(p.emit("vector", lambda v: v.memset(onesf[:, :], 1.0), (), "dv"))
        t_ms.append(p.emit("vector", lambda v: v.memset(onesb[:, :], 1.0), (), "dv"))
        t_ms.append(p.emit("vector", lambda v: v.memset(ones1x[:, :], 1.0), (), "dv"))
        t_ms.append(p.emit("vector", lambda v: v.memset(cEPS[:, :], EPS), (), "dv"))
        t_const = t_ms[-1]

        for wf_ in (AF.Sigmoid, AF.Ln, AF.Exp, AF.Gelu):
            ACT(s_sm[0:1, 0:1], cEPS[0:1, 0:1], wf_)

        t_par = {}
        for nm, dst in (("gnw", gnw_s), ("l1w", l1w_s), ("l1b", l1b_s),
                        ("l2w", l2w_s), ("l2b", l2b_s), ("eye128", eyes),
                        ("pmask", pmask_s)):
            t_par[nm] = DMA("sync", dst[:, :], ap[nm])
        t_par["eye8"] = DMA("sync", eye8s[:, :],
                            ap["eye8"].rearrange("a b c -> a (b c)"))
        t_rout = DMA("gpsimd", routb[:, :].rearrange("p (c e) -> p c e", c=DCH),
                     ap["router"].rearrange("c p e -> p c e"))

        for dc in range(DCH):
            src = ap["x0"][:, dc * 128:(dc + 1) * 128, :].rearrange("b p t -> p b t")
            dst = actA_c[dc].rearrange("p (b t) -> p b t", b=2)
            t_xb = DMA("gpsimd", dst, src)

        p.phase = 1
        t_scan_prev = None
        tG_prev = None
        t_ends_last = None
        t_fix_in = {}
        for oc in range(DCH):
            sl_f = load_slab(pack_src(ap["wmix"][0, oc]))
            sl_i = load_slab(pack_src(ap["wmix"][1, oc]))
            mmt = {}
            for tt in range(2):
                mmt[tt] = mm_accum(
                    B[tt], sl_f[1],
                    lambda kc, tt=tt: actA_c[kc][:, tt * 512:(tt + 1) * 512], DCH,
                    [sl_f[2], gate.get(("B", tt))])
            slab_done(sl_f[0], mmt[1])
            if oc == 0 and "mix0" in dbg_outs:
                ph = p.phase; p.phase = 0
                ACT(mix0_s[:, 0:512], B[0], AF.Copy, waits=[mmt[0]])
                p.phase = ph
            ta0 = ACT(a_sm[:, 0:512], B[0], AF.Sigmoid,
                      waits=[mmt[0], t_scan_prev])
            ta1 = ACT(a_sm[:, 512:1024], B[1], AF.Sigmoid, waits=[mmt[1]])
            gate[("B", 0)], gate[("B", 1)] = ta0, ta1
            for tt in range(2):
                mmt[2 + tt] = mm_accum(
                    B[2 + tt], sl_i[1],
                    lambda kc, tt=tt: actA_c[kc][:, tt * 512:(tt + 1) * 512], DCH,
                    [sl_i[2], gate.get(("B", 2 + tt))])
            slab_done(sl_i[0], mmt[3])
            tsi0 = ACT(s_sm[:, 0:512], B[2], AF.Sigmoid, waits=[mmt[2], tG_prev])
            tsi1 = ACT(s_sm[:, 512:1024], B[3], AF.Sigmoid, waits=[mmt[3]])
            tb = {}
            for tt in range(2):
                sl_ = slice(tt * 512, (tt + 1) * 512)
                t1 = TT(c_sm[:, sl_], B[2 + tt], s_sm[:, sl_], ALU.mult,
                        waits=[tsi0 if tt == 0 else tsi1, t_scan_prev])
                t2 = TS(d_sm[:, sl_], a_sm[:, sl_], -1.0, 1.0, ALU.mult, ALU.add,
                        waits=[ta0 if tt == 0 else ta1])
                tb[tt] = TT(c_sm[:, sl_], c_sm[:, sl_], d_sm[:, sl_], ALU.mult)
            gate[("B", 2)], gate[("B", 3)] = tb[0], tb[1]
            for b in range(2):
                sl_ = slice(b * 512, (b + 1) * 512)
                th = SCAN(bH_c[oc][:, sl_], a_sm[:, sl_], c_sm[:, sl_], 0.0,
                          ALU.add, waits=[tb[b]])
                tp_ = SCAN(bP_c[oc][:, sl_], a_sm[:, sl_], a_sm[:, sl_], 1.0,
                           ALU.bypass)
                col_h, col_p = oc * 2 + b, 32 + oc * 2 + b
                VCOPY(ends[:, col_h:col_h + 1], bH_c[oc][:, sl_.stop - 1:sl_.stop])
                t_ends_last = VCOPY(ends[:, col_p:col_p + 1],
                                    bP_c[oc][:, sl_.stop - 1:sl_.stop])
                t_fix_in[(oc, b)] = (th, tp_)
            if oc == 0 and "mix0" in dbg_outs:
                ph = p.phase; p.phase = 0
                m0 = dbg_outs["mix0"]
                VCOPY(mix0_s[:, 1024:2048], a_sm[:, :])
                tbd = VCOPY(mix0_s[:, 2048:3072], bH_c[0])
                DMA("sync", m0, mix0_s[:, :], waits=[tbd])
                p.phase = ph
            t_scan_prev = t_ends_last

        if "xb" in dbg_outs:
            ph = p.phase; p.phase = 0
            dump16("xb", actA_c, t_ends_last)
            p.phase = ph
        if "slab0" in dbg_outs:
            ph = p.phase; p.phase = 0
            DMA("sync", dbg_outs["slab0"], wsl[:, :], waits=[t_ends_last])
            p.phase = ph
        p.phase = 2
        p.flush_epoch("gpsimd")
        t_stage = DMA("gpsimd", cc_in.ap(), ends[:, :], waits=[t_ends_last])
        p.flush_epoch("gpsimd")
        p.emit("gpsimd",
               lambda g: g.collective_compute(
                   "AllGather", ALU.bypass,
                   replica_groups=[list(range(NCORE))],
                   ins=[cc_in.ap().opt()],
                   outs=[cc_out.ap().opt()]),
               waits=[t_stage], inc="cc")
        t_sg_prev = t_ends_last
        t_agg = None
        for oc in range(DCH):
            if oc == 3:
                t_agg = DMA("gpsimd", agg[:, :],
                            cc_out.ap().rearrange("(r p) c -> p r c", p=128),
                            waits=[Tok("cc", 1, "gpsimd")])
            sl_g = load_slab(pack_src(ap["wmix"][2, oc]))
            mmg = {}
            for tt in range(2):
                mmg[tt] = mm_accum(
                    B[4 + tt], sl_g[1],
                    lambda kc, tt=tt: actA_c[kc][:, tt * 512:(tt + 1) * 512], DCH,
                    [sl_g[2], gate.get(("B", 4 + tt))])
            slab_done(sl_g[0], mmg[1])
            tg0 = ACT(s_sm[:, 0:512], B[4], AF.Sigmoid, waits=[mmg[0], t_sg_prev])
            tg1 = ACT(s_sm[:, 512:1024], B[5], AF.Sigmoid, waits=[mmg[1]])
            tG0 = TT(actB_c[oc][:, 0:512], B[4], s_sm[:, 0:512], ALU.mult,
                     waits=[tg0])
            tG1 = TT(actB_c[oc][:, 512:1024], B[5], s_sm[:, 512:1024], ALU.mult,
                     waits=[tg1])
            gate[("B", 4)], gate[("B", 5)] = tG0, tG1
            t_sg_prev = tG1
        tG_prev = t_sg_prev
        aggv = agg[:, :].rearrange("p (r c) -> p r c", r=NCORE)
        if "aggsnap" in dbg_outs:
            ph = p.phase; p.phase = 0
            snapv = mix0_s[:, :].bitcast(F32)
            tsnap = VCOPY(snapv[:, 0:512], agg[:, :], waits=[t_agg])
            DMA("sync", dbg_outs["aggsnap"], snapv[:, 0:512], waits=[tsnap])
            p.phase = ph
        if "ccout" in dbg_outs:
            ph = p.phase; p.phase = 0
            DMA("sync", dbg_outs["ccout"], cc_out.ap(),
                waits=[Tok("cc", 1, "none")])
            p.phase = ph
        pre_r = [pre[:, r * 32:(r + 1) * 32] for r in range(NCORE)]
        carry = pre[:, 256:288]
        tpre = VCOPY(pre_r[0], aggv[:, 0, 0:32], waits=[t_agg])
        for r in range(1, NCORE):
            tmu_ = TT(pre[:, 288:320], pre_r[r - 1], aggv[:, r, 32:64], ALU.mult,
                      waits=[tpre])
            tpre = TT(pre_r[r], pre[:, 288:320], aggv[:, r, 0:32], ALU.add,
                      waits=[tmu_])
        if "aggsnap" in dbg_outs:
            ph = p.phase; p.phase = 0
            snapv2 = mix0_s[:, :].bitcast(F32)
            tsnap2 = VCOPY(snapv2[:, 512:832], pre[:, 0:320], waits=[(tpre, True)])
            DMA("sync", dbg_outs["presnap"], snapv2[:, 512:832], waits=[tsnap2])
            p.phase = ph
        tcar = p.emit("vector", lambda v: v.memset(carry, 0.0), (), "dv")
        for r in range(NCORE):
            tcar = STT(carry, pre_r[r], pmask_s[:, r:r + 1], carry, ALU.mult,
                       ALU.add, waits=[(tcar, True), t_par["pmask"]])
        t_fix = {}
        for oc in range(DCH):
            for b in range(2):
                sl_ = slice(b * 512, (b + 1) * 512)
                col = carry[:, oc * 2 + b:oc * 2 + b + 1]
                t_fix[(oc, b)] = STT(bH_c[oc][:, sl_], bP_c[oc][:, sl_], col,
                                     bH_c[oc][:, sl_], ALU.mult, ALU.add,
                                     waits=[(tcar, True)])
        t_fix_last = t_fix[(DCH - 1, 1)]


        dump16("H", bH_c, t_fix_last)
        dump16("P", bP_c, t_fix_last)
        if "ends" in dbg_outs:
            DMA("sync", dbg_outs["ends"], ends[:, :], waits=[t_ends_last])
        if "agg" in dbg_outs:
            DMA("sync", dbg_outs["agg"], agg[:, :], waits=[t_agg])
        if "carry" in dbg_outs:
            DMA("sync", dbg_outs["carry"], pre[:, :], waits=[(tcar, False)])

        p.phase = 3
        t_sq_mm = {0: None, 1: None}
        tq_prev = None
        for dc in range(DCH):
            tq = ACT(s_sm[:, :], bH_c[dc], AF.Square,
                     waits=[t_fix[(dc, 0)], t_fix[(dc, 1)],
                            t_sq_mm[1] if dc > 0 else None])
            for tt in range(2):
                bank = B[6][tt * 32:tt * 32 + 1, :]
                t_sq_mm[tt] = MM(bank, onesf[:, :],
                                 s_sm[:, tt * 512:(tt + 1) * 512],
                                 start=(dc == 0), stop=(dc == DCH - 1),
                                 waits=[tq, t_const, gate.get(("B6r", tt))]
                                 if dc == 0 else [tq])
            tq_prev = tq
        t_r1 = ACT(e_sm[0:1, 0:512], B[6][0:1, :], AF.Ln, scale=1.0 / D,
                   bias=cEPS[0:1, :], waits=[t_sq_mm[0], t_sq_mm[1]])
        t_r2 = ACT(e_sm[0:1, 512:1024], B[6][32:33, :], AF.Ln, scale=1.0 / D,
                   bias=cEPS[0:1, :])
        t_r3 = ACT(e_sm[0:1, 0:1024], e_sm[0:1, 0:1024], AF.Exp, scale=-0.5)
        gate[("B6r", 0)] = t_r1
        gate[("B6r", 1)] = t_r2
        t_b1 = MM(B[4], ones1x[:, :], e_sm[0:1, 0:512], True, True,
                  waits=[t_r3, gate[("B", 4)]])
        t_b2 = MM(B[5], ones1x[:, :], e_sm[0:1, 512:1024], True, True,
                  waits=[gate[("B", 5)]])
        t_e1 = VCOPY(e_sm[:, 0:512], B[4], waits=[t_b1, (t_r3, True)])
        t_e2 = VCOPY(e_sm[:, 512:1024], B[5], waits=[t_b2])
        gate[("B", 4)], gate[("B", 5)] = t_e1, t_e2
        t_of = {}
        for dc in range(DCH):
            t1 = STT(s_sm[:, :], bH_c[dc], gnw_s[:, dc:dc + 1], e_sm[:, :],
                     ALU.mult, ALU.mult,
                     waits=[t_e1, t_e2, t_par["gnw"], t_sq_mm[1]])
            t_of[dc] = TT(actA_c[dc], s_sm[:, :], actB_c[dc], ALU.mult)
        t_of_last = t_of[DCH - 1]
        dump16("ofinal", actA_c, t_of_last)

        p.phase = 4
        def ln_stats(dc, tt, src, sq, first, last, wf):
            bank_s = B[6][0:1, :] if tt == 0 else B[6][32:33, :]
            bank_q = B[6][64:65, :] if tt == 0 else B[7][0:1, :]
            ts_ = MM(bank_s, onesf[:, :], src, start=first, stop=last, waits=wf)
            tqq = MM(bank_q, onesf[:, :], sq, start=first, stop=last)
            return ts_, tqq

        def ln_rows(w_last):
            tmu0 = ACT(d_sm[0:1, 0:512], B[6][0:1, :], AF.Copy, scale=1.0 / D,
                       waits=list(w_last))
            tmu1 = ACT(d_sm[0:1, 512:1024], B[6][32:33, :], AF.Copy, scale=1.0 / D)
            tq0 = ACT(c_sm[0:1, 0:512], B[6][64:65, :], AF.Copy, scale=1.0 / D)
            tq1 = ACT(c_sm[0:1, 512:1024], B[7][0:1, :], AF.Copy, scale=1.0 / D)
            tv1 = TT(s_sm[0:1, :], d_sm[0:1, :], d_sm[0:1, :], ALU.mult,
                     waits=[tmu0, tmu1, tq0, tq1])
            tv2 = TT(s_sm[0:1, :], c_sm[0:1, :], s_sm[0:1, :], ALU.subtract)
            tl = ACT(e_sm[0:1, :], s_sm[0:1, :], AF.Ln, bias=cEPS[0:1, :],
                     waits=[tv2])
            te = ACT(e_sm[0:1, :], e_sm[0:1, :], AF.Exp, scale=-0.5)
            tb1 = MM(B[4], ones1x[:, :], d_sm[0:1, 0:512], True, True,
                     waits=[tmu1, te, gate.get(("B", 4))])
            tb2 = MM(B[5], ones1x[:, :], d_sm[0:1, 512:1024], True, True,
                     waits=[gate.get(("B", 5))])
            tc1 = VCOPY(d_sm[:, 0:512], B[4], waits=[tb1])
            tc2 = VCOPY(d_sm[:, 512:1024], B[5], waits=[tb2])
            tb3 = MM(B[4], ones1x[:, :], e_sm[0:1, 0:512], True, True,
                     waits=[te, tc1])
            tb4 = MM(B[5], ones1x[:, :], e_sm[0:1, 512:1024], True, True,
                     waits=[tc2])
            tc3 = VCOPY(e_sm[:, 0:512], B[4], waits=[tb3, (te, True)])
            tc4 = VCOPY(e_sm[:, 512:1024], B[5], waits=[tb4])
            gate[("B", 4)], gate[("B", 5)] = tc3, tc4
            gate[("B6r", 0)], gate[("B6r", 1)] = tmu0, tmu1
            gate[("B6r", 2)], gate[("B7r", 0)] = tq0, tq1
            return tc4

        t_xr = {}
        def xr_view(dc):
            s = dc % 2
            return actC[:, s * 2048:(s + 1) * 2048].bitcast(F32)

        def load_xr(dc, waits):
            src = ap["x0"][:, dc * 128:(dc + 1) * 128, :].rearrange("b p t -> p b t")
            dst = xr_view(dc).rearrange("p (b t) -> p b t", b=2)
            t_xr[dc] = DMA("sync", dst, src, waits=waits)

        load_xr(0, [t_par["eye8"]])
        load_xr(1, [])
        t_wo_ev = {}
        t_stats4_last = None
        for dc in range(DCH):
            sl_o = load_slab(pack_src(ap["wo"][dc]))
            mmt = {}
            for tt in range(2):
                mmt[tt] = mm_accum(
                    B[tt], sl_o[1],
                    lambda kc, tt=tt: actA_c[kc][:, tt * 512:(tt + 1) * 512], DCH,
                    [sl_o[2], t_of_last, gate.get(("B", tt))])
            slab_done(sl_o[0], mmt[1])
            xrv = xr_view(dc)
            for tt in range(2):
                sl_ = slice(tt * 512, (tt + 1) * 512)
                t1 = TT(resid_t(dc, tt), B[tt], xrv[:, sl_], ALU.add,
                        waits=[mmt[tt], t_xr[dc]])
                gate[("B", tt)] = t1
                t_wo_ev[(dc, tt)] = t1
                tqs = ACT(s_sm[:, sl_], resid_t(dc, tt), AF.Square,
                          waits=[t1, gate.get(("sq4", tt))])
                ts_, tqq = ln_stats(dc, tt, resid_t(dc, tt), s_sm[:, sl_],
                                    dc == 0, dc == DCH - 1,
                                    [t1, tqs] + ([gate.get(("B6r", tt)),
                                                  gate.get(("B6r", 2)),
                                                  gate.get(("B7r", 0))]
                                                 if dc == 0 else []))
                gate[("sq4", tt)] = tqq
                t_stats4_last = tqq
            if dc + 2 < DCH:
                load_xr(dc + 2, [t_wo_ev[(dc, 1)]])
        t_rows4 = ln_rows([t_stats4_last])
        t_x1 = {}
        for dc in range(DCH):
            for tt in range(2):
                sl_ = slice(tt * 512, (tt + 1) * 512)
                t1 = TT(a_sm[:, 0:512], resid_t(dc, tt), d_sm[:, sl_],
                        ALU.subtract, waits=[t_rows4] if dc == 0 and tt == 0 else ())
                t_x1[(dc, tt)] = TT(resid_t(dc, tt), a_sm[:, 0:512],
                                    e_sm[:, sl_], ALU.mult)
        t_x1_last = t_x1[(DCH - 1, 1)]
        if "x1" in dbg_outs:
            for dc in range(DCH):
                for tt in range(2):
                    DMA("sync", dbg_outs["x1"][:, dc * 1024 + tt * 512:
                                               dc * 1024 + (tt + 1) * 512],
                        resid_t(dc, tt), waits=[t_x1_last])

        p.phase = 5
        t_stats5_last = None
        for dc in range(DCH):
            for tt in range(2):
                sl_ = slice(tt * 512, (tt + 1) * 512)
                tqs = ACT(s_sm[:, sl_], resid_t(dc, tt), AF.Square,
                          waits=[t_x1[(dc, tt)], gate.get(("sq4", tt))])
                ts_, tqq = ln_stats(dc, tt, resid_t(dc, tt), s_sm[:, sl_],
                                    dc == 0, dc == DCH - 1,
                                    [t_x1[(dc, tt)], tqs] +
                                    ([gate.get(("B6r", 0)), gate.get(("B6r", 1)),
                                      gate.get(("B6r", 2)), gate.get(("B7r", 0))]
                                     if dc == 0 else []))
                gate[("sq4", tt)] = tqq
                t_stats5_last = tqq
        t_rows5 = ln_rows([t_stats5_last])
        t_ln1 = {}
        for dc in range(DCH):
            for tt in range(2):
                sl_ = slice(tt * 512, (tt + 1) * 512)
                TT(a_sm[:, 0:512], resid_t(dc, tt), d_sm[:, sl_], ALU.subtract,
                   waits=[t_rows5, tG_prev] if dc == 0 and tt == 0 else ())
                TT(a_sm[:, 0:512], a_sm[:, 0:512], e_sm[:, sl_], ALU.mult)
                t_ln1[(dc, tt)] = TS(actB_c[dc][:, sl_], a_sm[:, 0:512],
                                     l1w_s[:, dc:dc + 1], l1b_s[:, dc:dc + 1],
                                     ALU.mult, ALU.add,
                                     waits=[t_par["l1w"], t_par["l1b"]])
        t_ln1_last = t_ln1[(DCH - 1, 1)]
        dump16("ln1", actB_c, t_ln1_last)

        def q_tile(h):
            return actA[:, h * 512:(h + 1) * 512]
        def k_tile(g_):
            return actA[:, 8192 + g_ * 1024:8192 + (g_ + 1) * 1024]
        def v_tile(g_):
            return actA[:, 12288 + g_ * 1024:12288 + (g_ + 1) * 1024]

        t_qkv = {}
        for qc in range(QCH):
            g_, j = divmod(qc, 6)
            sl_q = load_slab(pack_src(ap["wqkv"][qc]))
            is_q = j <= 3
            for tt in range(2):
                if is_q and tt == 0:
                    continue
                bank = B[tt]
                mml = mm_accum(
                    bank, sl_q[1],
                    lambda kc, tt=tt: actB_c[kc][:, tt * 512:(tt + 1) * 512], DCH,
                    [sl_q[2], t_ln1_last, gate.get(("B", tt))])
                if is_q:
                    dst = q_tile(g_ * 4 + j)
                elif j == 4:
                    dst = k_tile(g_)[:, tt * 512:(tt + 1) * 512]
                else:
                    dst = v_tile(g_)[:, tt * 512:(tt + 1) * 512]
                tev = ACT(dst, bank, AF.Copy, waits=[mml, t_of_last])
                gate[("B", tt)] = tev
                t_qkv[(qc, tt)] = tev
                lastq = mml
            slab_done(sl_q[0], lastq)
        if "qkv" in dbg_outs:
            for i in range(DCH):
                DMA("sync", dbg_outs["qkv"][:, i * 1024:(i + 1) * 1024],
                    actA_c[i], waits=[t_qkv[(QCH - 1, 1)]])

        p.phase = 6
        kd_t = actC[:, 0:512]
        vd_t = actC[:, 512:1024]
        pr_t = actC[:, 1024:1536]
        t_attn_last = None
        t_pr_gate = None
        t_bc_gate = None
        for g_ in range(4):
            tkd = TT(kd_t, k_tile(g_)[:, 512:1024], k_tile(g_)[:, 0:512],
                     ALU.subtract,
                     waits=[t_qkv[(6 * g_ + 4, 0)], t_qkv[(6 * g_ + 4, 1)],
                            t_pr_gate])
            tvd = TT(vd_t, v_tile(g_)[:, 512:1024], v_tile(g_)[:, 0:512],
                     ALU.subtract,
                     waits=[t_qkv[(6 * g_ + 5, 0)], t_qkv[(6 * g_ + 5, 1)]])
            for j in range(4):
                h = g_ * 4 + j
                tpr = TT(pr_t, q_tile(h), kd_t, ALU.mult,
                         waits=[t_qkv[(6 * g_ + j, 1)], t_pr_gate])
                tdot = MM(B[6][0:1, :], onesb[:, :], pr_t, True, True,
                          waits=[tpr, t_const, gate.get("adot"),
                                 gate.get(("B6r", 0))])
                t_pr_gate = tdot
                trow = ACT(d_sm[0:1, 0:512], B[6][0:1, :], AF.Sigmoid,
                           scale=SCALE_ATT, waits=[tdot, t_bc_gate])
                gate["adot"] = trow
                tbc = MM(B[4], ones1x[:, :], d_sm[0:1, 0:512], True, True,
                         waits=[trow, gate.get(("B", 4))])
                t1 = TT(s_sm[:, 0:512], B[4], vd_t, ALU.mult, waits=[tbc])
                t_bc_gate = t1
                gate[("B", 4)] = t1
                t2 = TT(actB_c[h][:, 512:1024], v_tile(g_)[:, 0:512],
                        s_sm[:, 0:512], ALU.add)
                VCOPY(actB_c[h][:, 0:512], v_tile(g_)[:, 0:512])
                t_attn_last = t2
        t_attn_last = VCOPY(actB_c[15][:, 0:512], v_tile(3)[:, 0:512])
        dump16("attn", actB_c, t_attn_last)

        p.phase = 7
        t_stats7_last = None
        t_x2rb = {}
        t_acc = {}
        for dc in range(DCH):
            sl_p = load_slab(pack_src(ap["wproj"][dc]))
            mmt = {}
            for tt in range(2):
                mmt[tt] = mm_accum(
                    B[tt], sl_p[1],
                    lambda kc, tt=tt: actB_c[kc][:, tt * 512:(tt + 1) * 512], DCH,
                    [sl_p[2], t_attn_last, gate.get(("B", tt))])
            slab_done(sl_p[0], mmt[1])
            for tt in range(2):
                sl_ = slice(tt * 512, (tt + 1) * 512)
                tt1 = TT(c_sm[:, sl_], B[tt], resid_t(dc, tt), ALU.add,
                         waits=[mmt[tt], gate.get(("c7", tt))])
                gate[("B", tt)] = tt1
                tx = VCOPY(actA_c[dc][:, sl_], c_sm[:, sl_])
                t_x2rb[(dc, tt)] = tx
                ta = TT(resid_t(dc, tt), resid_t(dc, tt), c_sm[:, sl_], ALU.add)
                t_acc[(dc, tt)] = ta
                tqs = ACT(s_sm[:, sl_], c_sm[:, sl_], AF.Square,
                          waits=[tt1, gate.get(("sq4", tt))])
                ts_, tqq = ln_stats(dc, tt, c_sm[:, sl_], s_sm[:, sl_],
                                    dc == 0, dc == DCH - 1,
                                    [tt1, tqs, ta] +
                                    ([gate.get(("B6r", 0)), gate.get(("B6r", 1)),
                                      gate.get(("B6r", 2)), gate.get(("B7r", 0))]
                                     if dc == 0 else []))
                gate[("c7", tt)] = ts_
                gate[("sq4", tt)] = tqq
                t_stats7_last = tqq
        t_rows7 = ln_rows([t_stats7_last])
        t_ln2 = {}
        for dc in range(DCH):
            for tt in range(2):
                sl_ = slice(tt * 512, (tt + 1) * 512)
                TT(a_sm[:, 0:512], actA_c[dc][:, sl_], d_sm[:, sl_], ALU.subtract,
                   waits=[t_rows7, t_x2rb[(dc, tt)], t_attn_last]
                   if dc == 0 and tt == 0 else ())
                TT(a_sm[:, 0:512], a_sm[:, 0:512], e_sm[:, sl_], ALU.mult)
                t_ln2[(dc, tt)] = TS(actB_c[dc][:, sl_], a_sm[:, 0:512],
                                     l2w_s[:, dc:dc + 1], l2b_s[:, dc:dc + 1],
                                     ALU.mult, ALU.add,
                                     waits=[t_par["l2w"], t_par["l2b"]])
        t_ln2_last = t_ln2[(DCH - 1, 1)]
        dump16("ln2", actB_c, t_ln2_last)
        if "acc" in dbg_outs:
            for dc in range(DCH):
                for tt in range(2):
                    DMA("sync", dbg_outs["acc"][:, dc * 1024 + tt * 512:
                                                dc * 1024 + (tt + 1) * 512],
                        resid_t(dc, tt), waits=[t_acc[(DCH - 1, 1)]])

        p.phase = 8
        import os
        p8_level = int(os.environ.get("P8_LEVEL", "3"))
        gfm = d_sm
        t_log = {}
        for j in range(8):
            tt, lo = j // 4, (j % 4) * 128
            tok = None
            for dc in range(DCH):
                tok = MM(B[7][:, 8 * j:8 * j + 8],
                         actB_c[dc][:, tt * 512 + lo:tt * 512 + lo + 128],
                         routb[:, dc * 8:(dc + 1) * 8],
                         start=(dc == 0), stop=(dc == DCH - 1),
                         waits=[t_ln2_last, t_rout, gate.get(("B7r", 0))]
                         if j == 0 and dc == 0 else
                         ([t_ln2_last, t_rout] if dc == 0 else ()))
            t_log[j] = tok
        t_gate_tg2 = {}
        for j in range(8):
            m1 = gtm[:, 0:1]; m2 = gtm[:, 1:2]; den = gtm[:, 2:3]
            r8 = gtm[:, 8:16]; e8 = gtm[:, 16:24]; msk = gtm[:, 24:32]
            lg = gtm[:, 32:40]; gt8 = gtm[:, 48 + 8 * j:56 + 8 * j]
            tlg = VCOPY(lg, B[7][:, 8 * j:8 * j + 8], waits=[t_log[7], t_rows7])
            tm1 = VRED(m1, lg, ALU.max, waits=[tlg])
            ti = TS(r8, lg, m1, None, ALU.is_ge, waits=[tm1])
            tl = STT(r8, r8, -1e9, lg, ALU.mult, ALU.add, waits=[ti])
            tm2 = VRED(m2, r8, ALU.max, waits=[tl])
            te = TS(e8, lg, m1, None, ALU.subtract, waits=[tm1])
            tex = ACT(e8, e8, AF.Exp, waits=[te, gate.get("gact")])
            tmsk = TS(msk, lg, m2, None, ALU.is_ge, waits=[tm2])
            td1 = TT(den, m2, m1, ALU.subtract, waits=[tm2, tm1])
            td2 = ACT(den, den, AF.Exp, waits=[td1])
            gate["gact"] = td2
            td3 = TS(den, den, 1.0, None, ALU.add, waits=[td2])
            td4 = p.emit("vector", lambda v, den=den: v.reciprocal(out=den, in_=den),
                         [td3], "dv", fd=1)
            tg1 = TT(e8, e8, msk, ALU.mult, waits=[tex, tmsk])
            t_gate_tg2[j] = TS(gt8, e8, den, None, ALU.mult, waits=[td4, tg1])

        t_gate_last = None

        def emit_gate_transposes():
            nonlocal t_gate_last
            for j in range(8):
                gt8 = gtm[:, 48 + 8 * j:56 + 8 * j]
                ttr = TRANS(B[5][0:8, 0:128], gt8,
                            waits=[t_gate_tg2[j], t_par["eye128"],
                                   gate.get(("B", 5))])
                tcp = VCOPY(gfm[0:8, j * 128:(j + 1) * 128], B[5][0:8, 0:128],
                            waits=[ttr])
                gate[("B", 5)] = tcp
                t_gate_last = tcp
            if "gates" in dbg_outs:
                DMA("sync", dbg_outs["gates"], gfm[0:8, :], waits=[t_gate_last])

        p.phase = 9
        def h1_tile(fc):
            return actA_c[fc] if fc < DCH else actC_c[fc - DCH]

        t_h1_used = None
        t_gb_prev = None

        def emit_gb(e):
            nonlocal t_gb_prev
            tgb = {}
            for tt in range(2):
                tbk = MM(B[4 + tt], eye8s[0:8, e * 128:(e + 1) * 128],
                         gfm[0:8, tt * 512:(tt + 1) * 512], True, True,
                         waits=[t_gate_last, t_par["eye8"], gate.get(("B", 4 + tt))])
                tcp = VCOPY(e_sm[:, tt * 512:(tt + 1) * 512], B[4 + tt],
                            waits=[tbk, t_gb_prev])
                gate[("B", 4 + tt)] = tcp
                tgb[tt] = tcp
            t_gb_prev = tgb[1]
            return tgb

        for e in range(NE):
            if e > 0:
                tgb = emit_gb(e)
            tA_last = {}
            for fc in range(FCH):
                sl_1 = load_slab(pack_src(ap["w1"][e, fc]))
                mml = {}
                for tt in range(2):
                    bidx = 2 * (fc % 2) + tt
                    mml[tt] = mm_accum(
                        B[bidx], sl_1[1],
                        lambda kc, tt=tt: actB_c[kc][:, tt * 512:(tt + 1) * 512],
                        DCH, [sl_1[2], t_ln2_last, gate.get(("B", bidx))])
                    tev = ACT(h1_tile(fc)[:, tt * 512:(tt + 1) * 512], B[bidx],
                              AF.Gelu,
                              waits=[mml[tt], t_h1_used,
                                     (t_qkv[(QCH - 1, 1)] if e == 0 else None),
                                     (t_x2rb[(DCH - 1, 1)] if e == 0 else None)])
                    gate[("B", bidx)] = tev
                    tA_last[tt] = tev
                slab_done(sl_1[0], mml[1])
            if e == 0:
                emit_gate_transposes()
                tgb = emit_gb(0)
            tB_mm_last = None
            for dc in range(DCH):
                halves = []
                for hf in range(2):
                    srcv = ap["w2"][e, dc][:, hf * 16:(hf + 1) * 16, :]
                    halves.append(load_slab(pack_src(srcv)))
                mmt = {}
                for tt in range(2):
                    bidx = 2 * (dc % 2) + tt
                    tok = None
                    for hf in range(2):
                        sl_2 = halves[hf]
                        for kcl in range(DCH):
                            kc = hf * DCH + kcl
                            w = [sl_2[2], tA_last[0], tA_last[1],
                                 gate.get(("B", bidx))] if kcl == 0 else ()
                            tok = MM(B[bidx], slab_tile(sl_2[1], kcl),
                                     h1_tile(kc)[:, tt * 512:(tt + 1) * 512],
                                     start=(hf == 0 and kcl == 0),
                                     stop=(hf == 1 and kcl == DCH - 1), waits=w)
                    mmt[tt] = tok
                slab_done(halves[0][0], mmt[1])
                slab_done(halves[1][0], mmt[1])
                for tt in range(2):
                    sl_ = slice(tt * 512, (tt + 1) * 512)
                    t1 = TT(c_sm[:, sl_], B[2 * (dc % 2) + tt], e_sm[:, sl_],
                            ALU.mult,
                            waits=[mmt[tt], tgb[tt],
                                   (t_acc[(dc, tt)] if e == 0 else None),
                                   gate.get(("c9", tt))])
                    gate[("B", 2 * (dc % 2) + tt)] = t1
                    t2 = TT(resid_t(dc, tt), resid_t(dc, tt), c_sm[:, sl_],
                            ALU.add)
                    gate[("c9", tt)] = t2
                tB_mm_last = mmt[1]
            t_h1_used = tB_mm_last
        t_moe_last = gate[("c9", 1)]

        p.phase = 10
        t_stats10_last = None
        for dc in range(DCH):
            for tt in range(2):
                sl_ = slice(tt * 512, (tt + 1) * 512)
                tqs = ACT(s_sm[:, sl_], resid_t(dc, tt), AF.Square,
                          waits=[t_moe_last, gate.get(("sq4", tt))])
                ts_, tqq = ln_stats(dc, tt, resid_t(dc, tt), s_sm[:, sl_],
                                    dc == 0, dc == DCH - 1,
                                    [t_moe_last, tqs] +
                                    ([gate.get(("B6r", 0)), gate.get(("B6r", 1)),
                                      gate.get(("B6r", 2)), gate.get(("B7r", 0)),
                                      gate.get("adot")] if dc == 0 else []))
                gate[("sq4", tt)] = tqq
                t_stats10_last = tqq
        t_rows10 = ln_rows([t_stats10_last])
        t_st_gate = {}
        for dc in range(DCH):
            stage = s_sm if dc % 2 == 0 else c_sm
            for tt in range(2):
                sl_ = slice(tt * 512, (tt + 1) * 512)
                t1 = TT(a_sm[:, 0:512], resid_t(dc, tt), d_sm[:, sl_],
                        ALU.subtract,
                        waits=[t_rows10] if dc == 0 and tt == 0 else ())
                t2 = TT(stage[:, sl_], a_sm[:, 0:512], e_sm[:, sl_], ALU.mult,
                        waits=[t_st_gate.get((dc % 2, tt))])
                dst = yout[tt, dc * 128:(dc + 1) * 128, :]
                t_st_gate[(dc % 2, tt)] = DMA("sync", dst, stage[:, sl_],
                                              waits=[t2])
        p.phase = 0
        fin = []
        for q, pref in (("sync", "s"), ("gpsimd", "g")):
            eps = p.qepochs[q]
            for par in (0, 1):
                n = sum(eps[i][0] for i in range(len(eps)) if i % 2 == par)
                if n:
                    fin.append((Tok(pref + str(par), 16 * n, "none"), True))
        p.emit("sync", lambda sy: sy.nop(), waits=fin)

        p.replay(block, sems)

    return nc, dbg_outs



def _pack_lhsT(w):
    K, M = w.shape
    return np.ascontiguousarray(
        w.reshape(K // 128, 128, M // 128, 128).transpose(2, 1, 0, 3))


def _prep_inputs(inputs):
    f32 = np.float32
    x = np.asarray(inputs["L0_attn_hidden_states"], f32)
    xT = np.ascontiguousarray(x.transpose(0, 2, 1))

    wmix = np.stack([_pack_lhsT(np.asarray(inputs["L0_attn_Wf"], f32)),
                     _pack_lhsT(np.asarray(inputs["L0_attn_Wi"], f32)),
                     _pack_lhsT(np.asarray(inputs["L0_attn_Wg"], f32))])
    wo = _pack_lhsT(np.asarray(inputs["L0_attn_Wo"], f32))
    wqkv = _pack_lhsT(np.ascontiguousarray(
        np.asarray(inputs["L0_ffn_qkv_weight"], f32).T))
    wproj = _pack_lhsT(np.ascontiguousarray(
        np.asarray(inputs["L0_ffn_proj_weight"], f32).T))
    w1 = np.stack([_pack_lhsT(np.asarray(inputs["L0_ffn_moe_w1"], f32)[e])
                   for e in range(NE)])
    w2 = np.stack([_pack_lhsT(np.asarray(inputs["L0_ffn_moe_w2"], f32)[e])
                   for e in range(NE)])
    router = np.ascontiguousarray(
        np.asarray(inputs["L0_ffn_router_weight"], f32).reshape(DCH, 128, 8))

    def col(v):
        return np.ascontiguousarray(np.asarray(v, f32).reshape(DCH, 128).T)

    shared = {
        "wmix": wmix, "wo": wo, "wqkv": wqkv, "wproj": wproj,
        "w1": w1, "w2": w2, "router": router,
        "gnw": col(inputs["L0_attn_g_norm_weight"]),
        "l1w": col(inputs["L0_ffn_ln1_weight"]),
        "l1b": col(inputs["L0_ffn_ln1_bias"]),
        "l2w": col(inputs["L0_ffn_ln2_weight"]),
        "l2b": col(inputs["L0_ffn_ln2_bias"]),
        "eye128": np.eye(128, dtype=f32),
    }
    eye8 = np.zeros((8, 8, 128), f32)
    for e in range(8):
        eye8[e, e, :] = 1.0
    shared["eye8"] = eye8

    in_maps = []
    for c in range(NCORE):
        pm = np.zeros((128, 8), f32)
        if c > 0:
            pm[:, c - 1] = 1.0
        m = dict(shared)
        m["x0"] = np.ascontiguousarray(xT[:, :, c * TC:(c + 1) * TC])
        m["pmask"] = pm
        in_maps.append(m)
    return in_maps


def kernel(**inputs):
    in_maps = _prep_inputs(inputs)
    nc, _ = build_program()
    res = run_bass_kernel_spmd(nc, in_maps, core_ids=list(range(NCORE)))
    out = np.concatenate([res.results[c]["y"] for c in range(NCORE)], axis=2)
    return np.ascontiguousarray(out.transpose(0, 2, 1)).astype(np.float32)


# version 05
# speedup vs baseline: 1.0804x; 1.0195x over previous
import numpy as np
from contextlib import ExitStack

import concourse.bass as bass
import concourse.mybir as mybir
from concourse.bass_utils import run_bass_kernel_spmd

F32 = mybir.dt.float32
BF16 = mybir.dt.bfloat16
AF = mybir.ActivationFunctionType
ALU = mybir.AluOpType
AX = mybir.AxisListType

NCORE = 8
D = 2048
DCH = 16
TC = 512
FCH = 32
QCH = 24
NE = 8
EPS = 1e-5
SCALE_ATT = float(1.0 / np.sqrt(np.float32(128.0)))

ENGINES = ("tensor", "vector", "scalar", "gpsimd", "sync")
DMA_SEMS = frozenset(("g0", "g1", "g2", "s0", "s1", "s2", "cc"))
EPOCH = 2


class Tok:
    __slots__ = ("sem", "val", "eng", "fd")
    def __init__(self, sem, val, eng, fd=1 << 30):
        self.sem, self.val, self.eng, self.fd = sem, val, eng, fd


class EpochTok:
    __slots__ = ("queue", "epoch", "sem", "eng")
    def __init__(self, queue, epoch, sem, eng):
        self.queue, self.epoch, self.sem, self.eng = queue, epoch, sem, eng
    @property
    def val(self):
        raise RuntimeError("EpochTok.val is resolved at replay")


class Prog:
    def __init__(self, upto=99):
        self.ops = {e: [] for e in ENGINES}
        self.count = {}
        self.phase = 0
        self.upto = upto
        self.qepochs = {"gpsimd": [], "sync": []}

    def flush_epoch(self, queue):
        eps = self.qepochs[queue]
        if eps and eps[-1][0] < EPOCH:
            eps[-1][0] = eps[-1][0]
            eps[-1].append("closed")

    def dma_token(self, queue, eng):
        eps = self.qepochs[queue]
        if not eps or eps[-1][0] >= EPOCH or len(eps[-1]) > 1:
            eps.append([0])
        e = len(eps) - 1
        eps[-1][0] += 1
        sem = ("g" if queue == "gpsimd" else "s") + str(e % 3)
        gate = None
        if eps[-1][0] == 1 and e >= 3:
            gsem = ("g" if queue == "gpsimd" else "s") + str((e - 3) % 3)
            gate = EpochTok(queue, e - 3, gsem, "none")
        return EpochTok(queue, e, sem, eng), gate

    def resolve(self, tok):
        if isinstance(tok, EpochTok):
            eps = self.qepochs[tok.queue]
            n = sum(eps[i][0] for i in range(tok.epoch + 1)
                    if (("g" if tok.queue == "gpsimd" else "s") + str(i % 3)) == tok.sem)
            return tok.sem, 16 * n
        return tok.sem, tok.val

    def emit(self, eng, fn, waits=(), inc=None, inc_n=1, fd=1 << 30):
        if self.phase > self.upto:
            c = self.count.get(inc, 0) if inc is not None else 0
            return Tok(inc, c, eng, fd) if inc is not None else None
        wl = []
        for w in waits:
            force = False
            if isinstance(w, tuple):
                w, force = w
            if w is None:
                continue
            if isinstance(w, EpochTok) or w.sem in DMA_SEMS:
                force = True
            elif w.eng == eng and w.fd < 512:
                force = True
            if (not force) and w.eng == eng:
                continue
            wl.append(w)
        tok = None
        if inc is not None:
            c = self.count.get(inc, 0) + inc_n
            self.count[inc] = c
            tok = Tok(inc, c, eng, fd)
        self.ops[eng].append((wl, fn, inc, inc_n))
        return tok

    def emit_dma(self, queue, fn, waits=()):
        if self.phase > self.upto:
            eps = self.qepochs[queue]
            e = max(len(eps) - 1, 0)
            sem = ("g" if queue == "gpsimd" else "s") + str(e % 3)
            return EpochTok(queue, e, sem, queue)
        tok, gate = self.dma_token(queue, queue)
        wl = list(waits)
        if gate is not None:
            wl.append(gate)
        def fn2(g, fn=fn, sem=tok.sem):
            return fn(g, sem)
        self.ops[queue].append((
            [x for x in wl if x is not None], fn2, ("dma", tok.sem), 16))
        return tok

    def replay(self, block, sems):
        def mk(eng_name):
            def body(eng):
                seen = {}
                for wl, fn, inc, inc_n in self.ops[eng_name]:
                    for w in wl:
                        force = False
                        if isinstance(w, tuple):
                            w, force = w
                        if w is None:
                            continue
                        sem, val = self.resolve(w)
                        if seen.get(sem, -1) >= val:
                            continue
                        eng.wait_ge(sems[sem], val)
                        seen[sem] = val
                    inst = fn(eng)
                    if inc is not None:
                        if isinstance(inc, tuple):
                            inst.then_inc(sems[inc[1]], 16)
                        else:
                            inst.then_inc(sems[inc], inc_n)
            return body
        block.tensor(mk("tensor"))
        block.vector(mk("vector"))
        block.scalar(mk("scalar"))
        block.gpsimd(mk("gpsimd"))
        block.sync(mk("sync"))


def build_program(dbg=(), upto=99):
    dbg = set(dbg)
    nc = bass.Bass("TRN2", target_bir_lowering=False)

    ap = {}
    def din(name, shape):
        ap[name] = nc.dram_tensor(name, shape, F32, kind="ExternalInput").ap()

    din("x0", [2, D, TC])
    din("wmix", [3, DCH, 128, DCH, 128])
    din("wo", [DCH, 128, DCH, 128])
    din("wqkv", [QCH, 128, DCH, 128])
    din("wproj", [DCH, 128, DCH, 128])
    din("w1", [NE, FCH, 128, DCH, 128])
    din("w2", [NE, DCH, 128, FCH, 128])
    din("router", [DCH, 128, 8])
    din("gnw", [128, DCH])
    din("l1w", [128, DCH]); din("l1b", [128, DCH])
    din("l2w", [128, DCH]); din("l2b", [128, DCH])
    din("eye128", [128, 128])
    din("eye8", [8, 8, 128])
    din("pmask", [128, 8])

    yout = nc.dram_tensor("y", [2, D, TC], F32, kind="ExternalOutput").ap()
    dbg_outs = {}
    def mkdbg(name, shape, dtype=F32):
        if name in dbg:
            dbg_outs[name] = nc.dram_tensor("dbg_" + name, shape, dtype,
                                            kind="ExternalOutput").ap()

    mkdbg("a", [128, DCH * 1024])
    mkdbg("bsc", [128, DCH * 1024], BF16)
    mkdbg("H", [128, DCH * 1024], BF16)
    mkdbg("P", [128, DCH * 1024], BF16)
    mkdbg("ofinal", [128, DCH * 1024], BF16)
    mkdbg("x1", [128, DCH * 1024])
    mkdbg("ln1", [128, DCH * 1024], BF16)
    mkdbg("qkv", [128, DCH * 1024], BF16)
    mkdbg("attn", [128, DCH * 1024], BF16)
    mkdbg("ln2", [128, DCH * 1024], BF16)
    mkdbg("gates", [8, 1024])
    mkdbg("acc", [128, DCH * 1024])
    mkdbg("ends", [128, 64])
    mkdbg("mix0", [128, 3072], BF16)
    mkdbg("aggsnap", [128, 512])
    mkdbg("presnap", [128, 320])
    mkdbg("ccout", [NCORE * 128, 64])
    mkdbg("xb", [128, DCH * 1024], BF16)
    mkdbg("slab0", [128, 3 * 2048], BF16)
    mkdbg("agg", [128, NCORE * 64])
    mkdbg("carry", [128, 320])

    cc_in = nc.dram_tensor("cc_in", [128, 64], F32)
    cc_out = nc.dram_tensor("cc_out", [NCORE * 128, 64], F32)

    p = Prog(upto=upto)

    with ExitStack() as stk:
        def sbt(name, shape, dtype):
            return stk.enter_context(nc.sbuf_tensor(name, shape, dtype))

        bH = sbt("bH", [128, DCH * 1024], BF16)
        bP = sbt("bP", [128, DCH * 1024], BF16)
        actA = sbt("actA", [128, DCH * 1024], BF16)
        actB = sbt("actB", [128, DCH * 1024], BF16)
        actC = sbt("actC", [128, DCH * 1024], BF16)
        wsl = sbt("wsl", [128, 4 * 2048], BF16)
        a_sm = sbt("a_sm", [128, 1024], F32)
        s_sm = sbt("s_sm", [128, 1024], F32)
        c_sm = sbt("c_sm", [128, 1024], F32)
        d_sm = sbt("d_sm", [128, 1024], F32)
        e_sm = sbt("e_sm", [128, 1024], F32)
        ends = sbt("ends", [128, 64], F32)
        agg = sbt("agg", [128, NCORE * 64], F32)
        pre = sbt("pre", [128, 320], F32)
        routb = sbt("routb", [128, DCH * 8], BF16)
        gtm = sbt("gtm", [128, 144], F32)
        onesf = sbt("onesf", [128, 1], F32)
        onesb = sbt("onesb", [128, 1], BF16)
        ones1x = sbt("ones1x", [1, 128], F32)
        eyes = sbt("eyes", [128, 128], F32)
        eye8s = sbt("eye8s", [8, 8 * 128], F32)
        gnw_s = sbt("gnw_s", [128, DCH], F32)
        l1w_s = sbt("l1w_s", [128, DCH], F32)
        l1b_s = sbt("l1b_s", [128, DCH], F32)
        l2w_s = sbt("l2w_s", [128, DCH], F32)
        l2b_s = sbt("l2b_s", [128, DCH], F32)
        pmask_s = sbt("pmask_s", [128, 8], F32)
        mix0_s = sbt("mix0_s", [128, 3072], BF16) if ("mix0" in dbg or "aggsnap" in dbg) else None
        cEPS = sbt("cEPS", [128, 1], F32)

        B_h = [stk.enter_context(nc.psum_tensor(f"B{i}", [128, 512], F32))
               for i in range(8)]
        B = [h[:, :] for h in B_h]

        sems = {}
        for s_name in ("g0", "g1", "g2", "s0", "s1", "s2", "pe", "dv", "ac", "cc"):
            sems[s_name] = stk.enter_context(nc.semaphore(s_name))
        with nc.Block("semclear") as blk0:
            def _clear(v):
                for s in sems.values():
                    v.sem_clear(s)
            blk0.vector(_clear)
        block = stk.enter_context(nc.Block())

        def chunks(t):
            return [t[:, i * 1024:(i + 1) * 1024] for i in range(t.shape[1] // 1024)]

        bH_c, bP_c = chunks(bH), chunks(bP)
        actA_c, actB_c, actC_c = chunks(actA), chunks(actB), chunks(actC)
        residf = [t.bitcast(F32) for t in (bH_c + bP_c)]

        def resid_t(dc, tt):
            return residf[2 * dc + tt]

        xrC = [t.bitcast(F32) for t in actC_c]

        wslot = [wsl[:, i * 2048:(i + 1) * 2048] for i in range(4)]
        wslot.append(a_sm[:, :].bitcast(BF16))
        wslot.append(s_sm[:, :].bitcast(BF16))

        def slab_tile(slot, kc):
            return wslot[slot][:, kc * 128:(kc + 1) * 128]

        def TT(out, i0, i1, op, waits=(), inc="dv"):
            return p.emit("vector",
                          lambda v, out=out, i0=i0, i1=i1, op=op:
                          v.tensor_tensor(out=out, in0=i0, in1=i1, op=op),
                          waits, inc, fd=out.free_size())

        def TS(out, i0, s1v, s2v, op0, op1=None, waits=(), inc="dv"):
            def f(v, out=out, i0=i0, s1v=s1v, s2v=s2v, op0=op0, op1=op1):
                if op1 is None:
                    return v.tensor_scalar(out=out, in0=i0, scalar1=s1v,
                                           scalar2=None, op0=op0)
                return v.tensor_scalar(out=out, in0=i0, scalar1=s1v,
                                       scalar2=s2v, op0=op0, op1=op1)
            return p.emit("vector", f, waits, inc, fd=out.free_size())

        def STT(out, i0, sc, i1, op0, op1, waits=(), inc="dv"):
            return p.emit("vector",
                          lambda v, out=out, i0=i0, sc=sc, i1=i1, op0=op0, op1=op1:
                          v.scalar_tensor_tensor(out=out, in0=i0, scalar=sc,
                                                 in1=i1, op0=op0, op1=op1),
                          waits, inc, fd=out.free_size())

        def VCOPY(out, i0, waits=(), inc="dv"):
            return p.emit("vector",
                          lambda v, out=out, i0=i0: v.tensor_copy(out, i0),
                          waits, inc, fd=out.free_size())

        def VRED(out, i0, op, waits=(), inc="dv"):
            return p.emit("vector",
                          lambda v, out=out, i0=i0, op=op:
                          v.tensor_reduce(out=out, in_=i0, axis=AX.X, op=op),
                          waits, inc, fd=out.free_size())

        def SCAN(out, d0, d1, init, op1, waits=(), inc="dv"):
            return p.emit("vector",
                          lambda v, out=out, d0=d0, d1=d1, init=init, op1=op1:
                          v.tensor_tensor_scan(out=out, data0=d0, data1=d1,
                                               initial=init, op0=ALU.mult, op1=op1),
                          waits, inc, fd=out.free_size())

        def ACT(out, i0, func, scale=None, bias=None, waits=(), inc="ac"):
            def f(s, out=out, i0=i0, func=func, scale=scale, bias=bias):
                kw = {}
                if scale is not None:
                    kw["scale"] = scale
                if bias is not None:
                    kw["bias"] = bias
                return s.activation(out=out, in_=i0, func=func, **kw)
            return p.emit("scalar", f, waits, inc, fd=out.free_size())

        def MM(out, lhsT, rhs, start, stop, waits=(), inc="pe"):
            return p.emit("tensor",
                          lambda t, out=out, lhsT=lhsT, rhs=rhs, start=start, stop=stop:
                          t.matmul(out, lhsT=lhsT, rhs=rhs, start=start, stop=stop),
                          waits, inc)

        def TRANS(out, in_, waits=(), inc="pe"):
            return p.emit("tensor",
                          lambda t, out=out, in_=in_:
                          t.transpose(out, in_=in_, identity=eyes[:, :]),
                          waits, inc)

        def DMA(queue, out, in_, waits=(), inc=None):
            return p.emit_dma(queue,
                              lambda g, sem, out=out, in_=in_:
                              g.dma_start(out=out, in_=in_),
                              waits)

        def dump16(name, src_chunks, wtok):
            if name in dbg_outs:
                ph = p.phase
                p.phase = 0
                for i in range(DCH):
                    DMA("sync", dbg_outs[name][:, i * 1024:(i + 1) * 1024],
                        src_chunks[i], waits=[wtok])
                p.phase = ph

        gate = {}

        slab = {"idx": 0, "use": {}}

        slab_ring = [4]
        slab_slot_extra = {}

        def load_slab(src_block):
            i = slab["idx"]
            n = slab_ring[0]
            slot = i % n
            waits = [slab["use"].get(i - n)]
            waits += slab_slot_extra.pop(slot, [])
            tok = DMA("gpsimd", wslot[slot], src_block, waits=waits)
            slab["idx"] = i + 1
            return i, slot, tok

        def slab_done(i, tok):
            slab["use"][i] = tok

        def pack_src(base_ap):
            return base_ap.rearrange("p a b -> p (a b)")

        def mm_accum(bank, slot, rhs_of_kc, nk, waits_first, waits_kc=None):
            tok = None
            for kc in range(nk):
                w = list(waits_first) if kc == 0 else []
                if waits_kc is not None:
                    w += waits_kc(kc)
                tok = MM(bank, slab_tile(slot, kc), rhs_of_kc(kc),
                         start=(kc == 0), stop=(kc == nk - 1), waits=w)
            return tok

        t_ms = []
        t_ms.append(p.emit("vector", lambda v: v.memset(onesf[:, :], 1.0), (), "dv"))
        t_ms.append(p.emit("vector", lambda v: v.memset(onesb[:, :], 1.0), (), "dv"))
        t_ms.append(p.emit("vector", lambda v: v.memset(ones1x[:, :], 1.0), (), "dv"))
        t_ms.append(p.emit("vector", lambda v: v.memset(cEPS[:, :], EPS), (), "dv"))
        t_const = t_ms[-1]

        for wf_ in (AF.Sigmoid, AF.Ln, AF.Exp, AF.Gelu):
            ACT(s_sm[0:1, 0:1], cEPS[0:1, 0:1], wf_)

        t_par = {}
        for nm, dst in (("gnw", gnw_s), ("l1w", l1w_s), ("l1b", l1b_s),
                        ("l2w", l2w_s), ("l2b", l2b_s), ("eye128", eyes),
                        ("pmask", pmask_s)):
            t_par[nm] = DMA("sync", dst[:, :], ap[nm])
        t_par["eye8"] = DMA("sync", eye8s[:, :],
                            ap["eye8"].rearrange("a b c -> a (b c)"))
        t_rout = DMA("gpsimd", routb[:, :].rearrange("p (c e) -> p c e", c=DCH),
                     ap["router"].rearrange("c p e -> p c e"))

        for dc in range(DCH):
            src = ap["x0"][:, dc * 128:(dc + 1) * 128, :].rearrange("b p t -> p b t")
            dst = actA_c[dc].rearrange("p (b t) -> p b t", b=2)
            t_xb = DMA("gpsimd", dst, src)

        p.phase = 1
        t_scan_prev = None
        tG_prev = None
        t_ends_last = None
        t_fix_in = {}
        for oc in range(DCH):
            sl_f = load_slab(pack_src(ap["wmix"][0, oc]))
            sl_i = load_slab(pack_src(ap["wmix"][1, oc]))
            mmt = {}
            for tt in range(2):
                mmt[tt] = mm_accum(
                    B[tt], sl_f[1],
                    lambda kc, tt=tt: actA_c[kc][:, tt * 512:(tt + 1) * 512], DCH,
                    [sl_f[2], gate.get(("B", tt))])
            slab_done(sl_f[0], mmt[1])
            if oc == 0 and "mix0" in dbg_outs:
                ph = p.phase; p.phase = 0
                ACT(mix0_s[:, 0:512], B[0], AF.Copy, waits=[mmt[0]])
                p.phase = ph
            ta0 = ACT(a_sm[:, 0:512], B[0], AF.Sigmoid,
                      waits=[mmt[0], t_scan_prev])
            ta1 = ACT(a_sm[:, 512:1024], B[1], AF.Sigmoid, waits=[mmt[1]])
            gate[("B", 0)], gate[("B", 1)] = ta0, ta1
            for tt in range(2):
                mmt[2 + tt] = mm_accum(
                    B[2 + tt], sl_i[1],
                    lambda kc, tt=tt: actA_c[kc][:, tt * 512:(tt + 1) * 512], DCH,
                    [sl_i[2], gate.get(("B", 2 + tt))])
            slab_done(sl_i[0], mmt[3])
            tsi0 = ACT(s_sm[:, 0:512], B[2], AF.Sigmoid, waits=[mmt[2], tG_prev])
            tsi1 = ACT(s_sm[:, 512:1024], B[3], AF.Sigmoid, waits=[mmt[3]])
            tb = {}
            for tt in range(2):
                sl_ = slice(tt * 512, (tt + 1) * 512)
                t1 = TT(c_sm[:, sl_], B[2 + tt], s_sm[:, sl_], ALU.mult,
                        waits=[tsi0 if tt == 0 else tsi1, t_scan_prev])
                t2 = TS(d_sm[:, sl_], a_sm[:, sl_], -1.0, 1.0, ALU.mult, ALU.add,
                        waits=[ta0 if tt == 0 else ta1])
                tb[tt] = TT(c_sm[:, sl_], c_sm[:, sl_], d_sm[:, sl_], ALU.mult)
            gate[("B", 2)], gate[("B", 3)] = tb[0], tb[1]
            for b in range(2):
                sl_ = slice(b * 512, (b + 1) * 512)
                th = SCAN(bH_c[oc][:, sl_], a_sm[:, sl_], c_sm[:, sl_], 0.0,
                          ALU.add, waits=[tb[b]])
                tp_ = SCAN(bP_c[oc][:, sl_], a_sm[:, sl_], a_sm[:, sl_], 1.0,
                           ALU.bypass)
                col_h, col_p = oc * 2 + b, 32 + oc * 2 + b
                VCOPY(ends[:, col_h:col_h + 1], bH_c[oc][:, sl_.stop - 1:sl_.stop])
                t_ends_last = VCOPY(ends[:, col_p:col_p + 1],
                                    bP_c[oc][:, sl_.stop - 1:sl_.stop])
                t_fix_in[(oc, b)] = (th, tp_)
            if oc == 0 and "mix0" in dbg_outs:
                ph = p.phase; p.phase = 0
                m0 = dbg_outs["mix0"]
                VCOPY(mix0_s[:, 1024:2048], a_sm[:, :])
                tbd = VCOPY(mix0_s[:, 2048:3072], bH_c[0])
                DMA("sync", m0, mix0_s[:, :], waits=[tbd])
                p.phase = ph
            t_scan_prev = t_ends_last

        if "xb" in dbg_outs:
            ph = p.phase; p.phase = 0
            dump16("xb", actA_c, t_ends_last)
            p.phase = ph
        if "slab0" in dbg_outs:
            ph = p.phase; p.phase = 0
            DMA("sync", dbg_outs["slab0"], wsl[:, :], waits=[t_ends_last])
            p.phase = ph
        p.phase = 2
        sl_gs = {}
        for oc in (0, 1):
            sl_gs[oc] = load_slab(pack_src(ap["wmix"][2, oc]))
        p.flush_epoch("gpsimd")
        t_stage = DMA("gpsimd", cc_in.ap(), ends[:, :], waits=[t_ends_last])
        p.flush_epoch("gpsimd")
        p.emit("gpsimd",
               lambda g: g.collective_compute(
                   "AllGather", ALU.bypass,
                   replica_groups=[list(range(NCORE))],
                   ins=[cc_in.ap().opt()],
                   outs=[cc_out.ap().opt()]),
               waits=[t_stage], inc="cc")
        t_sg_prev = t_ends_last
        t_agg = None
        for oc in range(DCH):
            if oc + 2 < DCH:
                sl_gs[oc + 2] = load_slab(pack_src(ap["wmix"][2, oc + 2]))
            if oc == 3:
                t_agg = DMA("gpsimd", agg[:, :],
                            cc_out.ap().rearrange("(r p) c -> p r c", p=128),
                            waits=[Tok("cc", 1, "gpsimd")])
            sl_g = sl_gs[oc]
            mmg = {}
            for tt in range(2):
                mmg[tt] = mm_accum(
                    B[4 + tt], sl_g[1],
                    lambda kc, tt=tt: actA_c[kc][:, tt * 512:(tt + 1) * 512], DCH,
                    [sl_g[2], gate.get(("B", 4 + tt))])
            slab_done(sl_g[0], mmg[1])
            tg0 = ACT(s_sm[:, 0:512], B[4], AF.Sigmoid, waits=[mmg[0], t_sg_prev])
            tg1 = ACT(s_sm[:, 512:1024], B[5], AF.Sigmoid, waits=[mmg[1]])
            tG0 = TT(actB_c[oc][:, 0:512], B[4], s_sm[:, 0:512], ALU.mult,
                     waits=[tg0])
            tG1 = TT(actB_c[oc][:, 512:1024], B[5], s_sm[:, 512:1024], ALU.mult,
                     waits=[tg1])
            gate[("B", 4)], gate[("B", 5)] = tG0, tG1
            t_sg_prev = tG1
        tG_prev = t_sg_prev
        aggv = agg[:, :].rearrange("p (r c) -> p r c", r=NCORE)
        if "aggsnap" in dbg_outs:
            ph = p.phase; p.phase = 0
            snapv = mix0_s[:, :].bitcast(F32)
            tsnap = VCOPY(snapv[:, 0:512], agg[:, :], waits=[t_agg])
            DMA("sync", dbg_outs["aggsnap"], snapv[:, 0:512], waits=[tsnap])
            p.phase = ph
        if "ccout" in dbg_outs:
            ph = p.phase; p.phase = 0
            DMA("sync", dbg_outs["ccout"], cc_out.ap(),
                waits=[Tok("cc", 1, "none")])
            p.phase = ph
        pre_r = [pre[:, r * 32:(r + 1) * 32] for r in range(NCORE)]
        carry = pre[:, 256:288]
        tpre = VCOPY(pre_r[0], aggv[:, 0, 0:32], waits=[t_agg])
        for r in range(1, NCORE):
            tmu_ = TT(pre[:, 288:320], pre_r[r - 1], aggv[:, r, 32:64], ALU.mult,
                      waits=[tpre])
            tpre = TT(pre_r[r], pre[:, 288:320], aggv[:, r, 0:32], ALU.add,
                      waits=[tmu_])
        if "aggsnap" in dbg_outs:
            ph = p.phase; p.phase = 0
            snapv2 = mix0_s[:, :].bitcast(F32)
            tsnap2 = VCOPY(snapv2[:, 512:832], pre[:, 0:320], waits=[(tpre, True)])
            DMA("sync", dbg_outs["presnap"], snapv2[:, 512:832], waits=[tsnap2])
            p.phase = ph
        tcar = p.emit("vector", lambda v: v.memset(carry, 0.0), (), "dv")
        for r in range(NCORE):
            tcar = STT(carry, pre_r[r], pmask_s[:, r:r + 1], carry, ALU.mult,
                       ALU.add, waits=[(tcar, True), t_par["pmask"]])
        t_fix = {}
        for oc in range(DCH):
            for b in range(2):
                sl_ = slice(b * 512, (b + 1) * 512)
                col = carry[:, oc * 2 + b:oc * 2 + b + 1]
                t_fix[(oc, b)] = STT(bH_c[oc][:, sl_], bP_c[oc][:, sl_], col,
                                     bH_c[oc][:, sl_], ALU.mult, ALU.add,
                                     waits=[(tcar, True)])
        t_fix_last = t_fix[(DCH - 1, 1)]


        dump16("H", bH_c, t_fix_last)
        dump16("P", bP_c, t_fix_last)
        if "ends" in dbg_outs:
            DMA("sync", dbg_outs["ends"], ends[:, :], waits=[t_ends_last])
        if "agg" in dbg_outs:
            DMA("sync", dbg_outs["agg"], agg[:, :], waits=[t_agg])
        if "carry" in dbg_outs:
            DMA("sync", dbg_outs["carry"], pre[:, :], waits=[(tcar, False)])

        p.phase = 3
        t_sq_mm = {0: None, 1: None}
        tq_prev = None
        for dc in range(DCH):
            tq = ACT(s_sm[:, :], bH_c[dc], AF.Square,
                     waits=[t_fix[(dc, 0)], t_fix[(dc, 1)],
                            t_sq_mm[1] if dc > 0 else None])
            for tt in range(2):
                bank = B[6][tt * 32:tt * 32 + 1, :]
                t_sq_mm[tt] = MM(bank, onesf[:, :],
                                 s_sm[:, tt * 512:(tt + 1) * 512],
                                 start=(dc == 0), stop=(dc == DCH - 1),
                                 waits=[tq, t_const, gate.get(("B6r", tt))]
                                 if dc == 0 else [tq])
            tq_prev = tq
        t_r1 = ACT(e_sm[0:1, 0:512], B[6][0:1, :], AF.Ln, scale=1.0 / D,
                   bias=cEPS[0:1, :], waits=[t_sq_mm[0], t_sq_mm[1]])
        t_r2 = ACT(e_sm[0:1, 512:1024], B[6][32:33, :], AF.Ln, scale=1.0 / D,
                   bias=cEPS[0:1, :])
        t_r3 = ACT(e_sm[0:1, 0:1024], e_sm[0:1, 0:1024], AF.Exp, scale=-0.5)
        gate[("B6r", 0)] = t_r1
        gate[("B6r", 1)] = t_r2
        t_b1 = MM(B[4], ones1x[:, :], e_sm[0:1, 0:512], True, True,
                  waits=[t_r3, gate[("B", 4)]])
        t_b2 = MM(B[5], ones1x[:, :], e_sm[0:1, 512:1024], True, True,
                  waits=[gate[("B", 5)]])
        t_e1 = VCOPY(e_sm[:, 0:512], B[4], waits=[t_b1, (t_r3, True)])
        t_e2 = VCOPY(e_sm[:, 512:1024], B[5], waits=[t_b2])
        gate[("B", 4)], gate[("B", 5)] = t_e1, t_e2
        t_of = {}
        for dc in range(DCH):
            t1 = STT(s_sm[:, :], bH_c[dc], gnw_s[:, dc:dc + 1], e_sm[:, :],
                     ALU.mult, ALU.mult,
                     waits=[t_e1, t_e2, t_par["gnw"], t_sq_mm[1]])
            t_of[dc] = TT(actA_c[dc], s_sm[:, :], actB_c[dc], ALU.mult)
        t_of_last = t_of[DCH - 1]
        dump16("ofinal", actA_c, t_of_last)

        p.phase = 4
        def ln_stats(dc, tt, src, sq, first, last, wf):
            bank_s = B[6][0:1, :] if tt == 0 else B[6][32:33, :]
            bank_q = B[6][64:65, :] if tt == 0 else B[7][0:1, :]
            ts_ = MM(bank_s, onesf[:, :], src, start=first, stop=last, waits=wf)
            tqq = MM(bank_q, onesf[:, :], sq, start=first, stop=last)
            return ts_, tqq

        def ln_rows(w_last):
            tmu0 = ACT(d_sm[0:1, 0:512], B[6][0:1, :], AF.Copy, scale=1.0 / D,
                       waits=list(w_last))
            tmu1 = ACT(d_sm[0:1, 512:1024], B[6][32:33, :], AF.Copy, scale=1.0 / D)
            tq0 = ACT(c_sm[0:1, 0:512], B[6][64:65, :], AF.Copy, scale=1.0 / D)
            tq1 = ACT(c_sm[0:1, 512:1024], B[7][0:1, :], AF.Copy, scale=1.0 / D)
            tv1 = TT(s_sm[0:1, :], d_sm[0:1, :], d_sm[0:1, :], ALU.mult,
                     waits=[tmu0, tmu1, tq0, tq1])
            tv2 = TT(s_sm[0:1, :], c_sm[0:1, :], s_sm[0:1, :], ALU.subtract)
            tl = ACT(e_sm[0:1, :], s_sm[0:1, :], AF.Ln, bias=cEPS[0:1, :],
                     waits=[tv2])
            te = ACT(e_sm[0:1, :], e_sm[0:1, :], AF.Exp, scale=-0.5)
            tb1 = MM(B[4], ones1x[:, :], d_sm[0:1, 0:512], True, True,
                     waits=[tmu1, te, gate.get(("B", 4))])
            tb2 = MM(B[5], ones1x[:, :], d_sm[0:1, 512:1024], True, True,
                     waits=[gate.get(("B", 5))])
            tc1 = VCOPY(d_sm[:, 0:512], B[4], waits=[tb1])
            tc2 = VCOPY(d_sm[:, 512:1024], B[5], waits=[tb2])
            tb3 = MM(B[4], ones1x[:, :], e_sm[0:1, 0:512], True, True,
                     waits=[te, tc1])
            tb4 = MM(B[5], ones1x[:, :], e_sm[0:1, 512:1024], True, True,
                     waits=[tc2])
            tc3 = VCOPY(e_sm[:, 0:512], B[4], waits=[tb3, (te, True)])
            tc4 = VCOPY(e_sm[:, 512:1024], B[5], waits=[tb4])
            gate[("B", 4)], gate[("B", 5)] = tc3, tc4
            gate[("B6r", 0)], gate[("B6r", 1)] = tmu0, tmu1
            gate[("B6r", 2)], gate[("B7r", 0)] = tq0, tq1
            return tc4

        t_xr = {}
        def xr_view(dc):
            s = dc % 2
            return actC[:, s * 2048:(s + 1) * 2048].bitcast(F32)

        def load_xr(dc, waits):
            src = ap["x0"][:, dc * 128:(dc + 1) * 128, :].rearrange("b p t -> p b t")
            dst = xr_view(dc).rearrange("p (b t) -> p b t", b=2)
            t_xr[dc] = DMA("sync", dst, src, waits=waits)

        load_xr(0, [t_par["eye8"]])
        load_xr(1, [])
        t_wo_ev = {}
        t_stats4_last = None
        for dc in range(DCH):
            sl_o = load_slab(pack_src(ap["wo"][dc]))
            mmt = {}
            for tt in range(2):
                mmt[tt] = mm_accum(
                    B[tt], sl_o[1],
                    lambda kc, tt=tt: actA_c[kc][:, tt * 512:(tt + 1) * 512], DCH,
                    [sl_o[2], gate.get(("B", tt))],
                    waits_kc=(lambda kc: [t_of[kc]]) if dc == 0 else
                             (lambda kc: [t_of_last] if kc == 0 else []))
            slab_done(sl_o[0], mmt[1])
            xrv = xr_view(dc)
            for tt in range(2):
                sl_ = slice(tt * 512, (tt + 1) * 512)
                t1 = TT(resid_t(dc, tt), B[tt], xrv[:, sl_], ALU.add,
                        waits=[mmt[tt], t_xr[dc]])
                gate[("B", tt)] = t1
                t_wo_ev[(dc, tt)] = t1
                tqs = ACT(s_sm[:, sl_], resid_t(dc, tt), AF.Square,
                          waits=[t1, gate.get(("sq4", tt))])
                ts_, tqq = ln_stats(dc, tt, resid_t(dc, tt), s_sm[:, sl_],
                                    dc == 0, dc == DCH - 1,
                                    [t1, tqs] + ([gate.get(("B6r", tt)),
                                                  gate.get(("B6r", 2)),
                                                  gate.get(("B7r", 0))]
                                                 if dc == 0 else []))
                gate[("sq4", tt)] = tqq
                t_stats4_last = tqq
            if dc + 2 < DCH:
                load_xr(dc + 2, [t_wo_ev[(dc, 1)]])
        t_rows4 = ln_rows([t_stats4_last])
        t_x1 = {}
        for dc in range(DCH):
            for tt in range(2):
                sl_ = slice(tt * 512, (tt + 1) * 512)
                t1 = TT(a_sm[:, 0:512], resid_t(dc, tt), d_sm[:, sl_],
                        ALU.subtract, waits=[t_rows4] if dc == 0 and tt == 0 else ())
                t_x1[(dc, tt)] = TT(resid_t(dc, tt), a_sm[:, 0:512],
                                    e_sm[:, sl_], ALU.mult)
        t_x1_last = t_x1[(DCH - 1, 1)]
        if "x1" in dbg_outs:
            for dc in range(DCH):
                for tt in range(2):
                    DMA("sync", dbg_outs["x1"][:, dc * 1024 + tt * 512:
                                               dc * 1024 + (tt + 1) * 512],
                        resid_t(dc, tt), waits=[t_x1_last])

        p.phase = 5
        t_stats5_last = None
        for dc in range(DCH):
            for tt in range(2):
                sl_ = slice(tt * 512, (tt + 1) * 512)
                tqs = ACT(s_sm[:, sl_], resid_t(dc, tt), AF.Square,
                          waits=[t_x1[(dc, tt)], gate.get(("sq4", tt))])
                ts_, tqq = ln_stats(dc, tt, resid_t(dc, tt), s_sm[:, sl_],
                                    dc == 0, dc == DCH - 1,
                                    [t_x1[(dc, tt)], tqs] +
                                    ([gate.get(("B6r", 0)), gate.get(("B6r", 1)),
                                      gate.get(("B6r", 2)), gate.get(("B7r", 0))]
                                     if dc == 0 else []))
                gate[("sq4", tt)] = tqq
                t_stats5_last = tqq
        t_rows5 = ln_rows([t_stats5_last])
        t_ln1 = {}
        for dc in range(DCH):
            for tt in range(2):
                sl_ = slice(tt * 512, (tt + 1) * 512)
                TT(a_sm[:, 0:512], resid_t(dc, tt), d_sm[:, sl_], ALU.subtract,
                   waits=[t_rows5, tG_prev] if dc == 0 and tt == 0 else ())
                TT(a_sm[:, 0:512], a_sm[:, 0:512], e_sm[:, sl_], ALU.mult)
                t_ln1[(dc, tt)] = TS(actB_c[dc][:, sl_], a_sm[:, 0:512],
                                     l1w_s[:, dc:dc + 1], l1b_s[:, dc:dc + 1],
                                     ALU.mult, ALU.add,
                                     waits=[t_par["l1w"], t_par["l1b"]])
        t_ln1_last = t_ln1[(DCH - 1, 1)]
        dump16("ln1", actB_c, t_ln1_last)

        def q_tile(h):
            return actA[:, h * 512:(h + 1) * 512]
        def k_tile(g_):
            return actA[:, 8192 + g_ * 1024:8192 + (g_ + 1) * 1024]
        def v_tile(g_):
            return actA[:, 12288 + g_ * 1024:12288 + (g_ + 1) * 1024]

        t_qkv = {}
        for qc in range(QCH):
            g_, j = divmod(qc, 6)
            sl_q = load_slab(pack_src(ap["wqkv"][qc]))
            is_q = j <= 3
            for tt in range(2):
                if is_q and tt == 0:
                    continue
                bank = B[tt]
                mml = mm_accum(
                    bank, sl_q[1],
                    lambda kc, tt=tt: actB_c[kc][:, tt * 512:(tt + 1) * 512], DCH,
                    [sl_q[2], gate.get(("B", tt))],
                    waits_kc=(lambda kc, tt=tt: [t_ln1[(kc, tt)]]) if qc == 0 else
                             (lambda kc: [t_ln1_last] if kc == 0 else []))
                if is_q:
                    dst = q_tile(g_ * 4 + j)
                elif j == 4:
                    dst = k_tile(g_)[:, tt * 512:(tt + 1) * 512]
                else:
                    dst = v_tile(g_)[:, tt * 512:(tt + 1) * 512]
                tev = ACT(dst, bank, AF.Copy, waits=[mml, t_of_last])
                gate[("B", tt)] = tev
                t_qkv[(qc, tt)] = tev
                lastq = mml
            slab_done(sl_q[0], lastq)
        if "qkv" in dbg_outs:
            for i in range(DCH):
                DMA("sync", dbg_outs["qkv"][:, i * 1024:(i + 1) * 1024],
                    actA_c[i], waits=[t_qkv[(QCH - 1, 1)]])

        p.phase = 6
        kd_t = actC[:, 0:512]
        vd_t = actC[:, 512:1024]
        pr_t = actC[:, 1024:1536]
        t_attn_last = None
        t_pr_gate = None
        t_bc_gate = None
        for g_ in range(4):
            tkd = TT(kd_t, k_tile(g_)[:, 512:1024], k_tile(g_)[:, 0:512],
                     ALU.subtract,
                     waits=[t_qkv[(6 * g_ + 4, 0)], t_qkv[(6 * g_ + 4, 1)],
                            t_pr_gate])
            tvd = TT(vd_t, v_tile(g_)[:, 512:1024], v_tile(g_)[:, 0:512],
                     ALU.subtract,
                     waits=[t_qkv[(6 * g_ + 5, 0)], t_qkv[(6 * g_ + 5, 1)]])
            for j in range(4):
                h = g_ * 4 + j
                tpr = TT(pr_t, q_tile(h), kd_t, ALU.mult,
                         waits=[t_qkv[(6 * g_ + j, 1)], t_pr_gate])
                tdot = MM(B[6][0:1, :], onesb[:, :], pr_t, True, True,
                          waits=[tpr, t_const, gate.get("adot"),
                                 gate.get(("B6r", 0))])
                t_pr_gate = tdot
                trow = ACT(d_sm[0:1, 0:512], B[6][0:1, :], AF.Sigmoid,
                           scale=SCALE_ATT, waits=[tdot, t_bc_gate])
                gate["adot"] = trow
                tbc = MM(B[4], ones1x[:, :], d_sm[0:1, 0:512], True, True,
                         waits=[trow, gate.get(("B", 4))])
                t1 = TT(s_sm[:, 0:512], B[4], vd_t, ALU.mult, waits=[tbc])
                t_bc_gate = t1
                gate[("B", 4)] = t1
                t2 = TT(actB_c[h][:, 512:1024], v_tile(g_)[:, 0:512],
                        s_sm[:, 0:512], ALU.add)
                VCOPY(actB_c[h][:, 0:512], v_tile(g_)[:, 0:512])
                t_attn_last = t2
        t_attn_last = VCOPY(actB_c[15][:, 0:512], v_tile(3)[:, 0:512])
        dump16("attn", actB_c, t_attn_last)

        p.phase = 7
        t_stats7_last = None
        t_x2rb = {}
        t_acc = {}
        for dc in range(DCH):
            sl_p = load_slab(pack_src(ap["wproj"][dc]))
            mmt = {}
            for tt in range(2):
                mmt[tt] = mm_accum(
                    B[tt], sl_p[1],
                    lambda kc, tt=tt: actB_c[kc][:, tt * 512:(tt + 1) * 512], DCH,
                    [sl_p[2], t_attn_last, gate.get(("B", tt))])
            slab_done(sl_p[0], mmt[1])
            for tt in range(2):
                sl_ = slice(tt * 512, (tt + 1) * 512)
                tt1 = TT(c_sm[:, sl_], B[tt], resid_t(dc, tt), ALU.add,
                         waits=[mmt[tt], gate.get(("c7", tt))])
                gate[("B", tt)] = tt1
                tx = VCOPY(actA_c[dc][:, sl_], c_sm[:, sl_])
                t_x2rb[(dc, tt)] = tx
                ta = TT(resid_t(dc, tt), resid_t(dc, tt), c_sm[:, sl_], ALU.add)
                t_acc[(dc, tt)] = ta
                tqs = ACT(s_sm[:, sl_], c_sm[:, sl_], AF.Square,
                          waits=[tt1, gate.get(("sq4", tt))])
                ts_, tqq = ln_stats(dc, tt, c_sm[:, sl_], s_sm[:, sl_],
                                    dc == 0, dc == DCH - 1,
                                    [tt1, tqs, ta] +
                                    ([gate.get(("B6r", 0)), gate.get(("B6r", 1)),
                                      gate.get(("B6r", 2)), gate.get(("B7r", 0))]
                                     if dc == 0 else []))
                gate[("c7", tt)] = ts_
                gate[("sq4", tt)] = tqq
                t_stats7_last = tqq
        t_rows7 = ln_rows([t_stats7_last])
        t_ln2 = {}
        for dc in range(DCH):
            for tt in range(2):
                sl_ = slice(tt * 512, (tt + 1) * 512)
                TT(a_sm[:, 0:512], actA_c[dc][:, sl_], d_sm[:, sl_], ALU.subtract,
                   waits=[t_rows7, t_x2rb[(dc, tt)], t_attn_last]
                   if dc == 0 and tt == 0 else ())
                TT(a_sm[:, 0:512], a_sm[:, 0:512], e_sm[:, sl_], ALU.mult)
                t_ln2[(dc, tt)] = TS(actB_c[dc][:, sl_], a_sm[:, 0:512],
                                     l2w_s[:, dc:dc + 1], l2b_s[:, dc:dc + 1],
                                     ALU.mult, ALU.add,
                                     waits=[t_par["l2w"], t_par["l2b"]])
        t_ln2_last = t_ln2[(DCH - 1, 1)]
        dump16("ln2", actB_c, t_ln2_last)
        if "acc" in dbg_outs:
            for dc in range(DCH):
                for tt in range(2):
                    DMA("sync", dbg_outs["acc"][:, dc * 1024 + tt * 512:
                                                dc * 1024 + (tt + 1) * 512],
                        resid_t(dc, tt), waits=[t_acc[(DCH - 1, 1)]])

        p.phase = 8
        import os
        p8_level = int(os.environ.get("P8_LEVEL", "3"))
        gfm = d_sm
        t_log = {}
        for j in range(8):
            tt, lo = j // 4, (j % 4) * 128
            tok = None
            for dc in range(DCH):
                w = [t_ln2[(dc, tt)]] if j == 0 else []
                if j == 0 and dc == 0:
                    w += [t_rout, gate.get(("B7r", 0))]
                elif dc == 0 and j > 0:
                    w += [t_ln2_last, t_rout]
                tok = MM(B[7][:, 8 * j:8 * j + 8],
                         actB_c[dc][:, tt * 512 + lo:tt * 512 + lo + 128],
                         routb[:, dc * 8:(dc + 1) * 8],
                         start=(dc == 0), stop=(dc == DCH - 1), waits=w)
            t_log[j] = tok
        t_gate_tg2 = {}
        for j in range(8):
            m1 = gtm[:, 0:1]; m2 = gtm[:, 1:2]; den = gtm[:, 2:3]
            r8 = gtm[:, 8:16]; e8 = gtm[:, 16:24]; msk = gtm[:, 24:32]
            lg = gtm[:, 32:40]; gt8 = gtm[:, 48 + 8 * j:56 + 8 * j]
            tlg = VCOPY(lg, B[7][:, 8 * j:8 * j + 8], waits=[t_log[7], t_rows7])
            tm1 = VRED(m1, lg, ALU.max, waits=[tlg])
            ti = TS(r8, lg, m1, None, ALU.is_ge, waits=[tm1])
            tl = STT(r8, r8, -1e9, lg, ALU.mult, ALU.add, waits=[ti])
            tm2 = VRED(m2, r8, ALU.max, waits=[tl])
            te = TS(e8, lg, m1, None, ALU.subtract, waits=[tm1])
            tex = ACT(e8, e8, AF.Exp, waits=[te, gate.get("gact")])
            tmsk = TS(msk, lg, m2, None, ALU.is_ge, waits=[tm2])
            td1 = TT(den, m2, m1, ALU.subtract, waits=[tm2, tm1])
            td2 = ACT(den, den, AF.Exp, waits=[td1])
            gate["gact"] = td2
            td3 = TS(den, den, 1.0, None, ALU.add, waits=[td2])
            td4 = p.emit("vector", lambda v, den=den: v.reciprocal(out=den, in_=den),
                         [td3], "dv", fd=1)
            tg1 = TT(e8, e8, msk, ALU.mult, waits=[tex, tmsk])
            t_gate_tg2[j] = TS(gt8, e8, den, None, ALU.mult, waits=[td4, tg1])

        t_gate_last = None

        def emit_gate_transposes():
            nonlocal t_gate_last
            for j in range(8):
                gt8 = gtm[:, 48 + 8 * j:56 + 8 * j]
                ttr = TRANS(B[5][0:8, 0:128], gt8,
                            waits=[t_gate_tg2[j], t_par["eye128"],
                                   gate.get(("B", 5))])
                tcp = VCOPY(gfm[0:8, j * 128:(j + 1) * 128], B[5][0:8, 0:128],
                            waits=[ttr])
                gate[("B", 5)] = tcp
                t_gate_last = tcp
            if "gates" in dbg_outs:
                DMA("sync", dbg_outs["gates"], gfm[0:8, :], waits=[t_gate_last])

        p.phase = 9
        if int(os.environ.get("MOE_RING", "4")) == 6:
            _i0 = slab["idx"]
            _last = slab["use"].get(_i0 - 1)
            slab["idx"] = ((_i0 + 5) // 6) * 6
            for _k in range(6):
                slab["use"][slab["idx"] - 1 - _k] = _last
            slab_ring[0] = 6
            slab_slot_extra[4] = [t_ln2_last, gate.get(("sq4", 0)),
                                  gate.get(("sq4", 1))]
            slab_slot_extra[5] = [t_ln2_last, gate.get(("sq4", 0)),
                                  gate.get(("sq4", 1))]
        def h1_tile(fc):
            return actA_c[fc] if fc < DCH else actC_c[fc - DCH]

        t_h1_used = None
        t_gb_prev = None

        def emit_gb(e):
            nonlocal t_gb_prev
            tgb = {}
            for tt in range(2):
                tbk = MM(B[4 + tt], eye8s[0:8, e * 128:(e + 1) * 128],
                         gfm[0:8, tt * 512:(tt + 1) * 512], True, True,
                         waits=[t_gate_last, t_par["eye8"], gate.get(("B", 4 + tt))])
                tcp = VCOPY(e_sm[:, tt * 512:(tt + 1) * 512], B[4 + tt],
                            waits=[tbk, t_gb_prev])
                gate[("B", 4 + tt)] = tcp
                tgb[tt] = tcp
            t_gb_prev = tgb[1]
            return tgb

        ne_lim = int(os.environ.get('NE_LIM', str(NE)))
        for e in range(ne_lim):
            if e > 0:
                tgb = emit_gb(e)
            tA_last = {}
            for fc in range(FCH):
                sl_1 = load_slab(pack_src(ap["w1"][e, fc]))
                mml = {}
                for tt in range(2):
                    bidx = 2 * (fc % 2) + tt
                    mml[tt] = mm_accum(
                        B[bidx], sl_1[1],
                        lambda kc, tt=tt: actB_c[kc][:, tt * 512:(tt + 1) * 512],
                        DCH, [sl_1[2], gate.get(("B", bidx))],
                        waits_kc=(lambda kc, tt=tt: [t_ln2[(kc, tt)]])
                        if (e == 0 and fc == 0) else
                        (lambda kc: [t_ln2_last] if kc == 0 else []))
                    tev = ACT(h1_tile(fc)[:, tt * 512:(tt + 1) * 512], B[bidx],
                              AF.Gelu,
                              waits=[mml[tt], t_h1_used,
                                     (t_qkv[(QCH - 1, 1)] if e == 0 else None),
                                     (t_x2rb[(DCH - 1, 1)] if e == 0 else None)])
                    gate[("B", bidx)] = tev
                    tA_last[tt] = tev
                slab_done(sl_1[0], mml[1])
            if e == 0:
                emit_gate_transposes()
                tgb = emit_gb(0)
            tB_mm_last = None
            for dc in range(DCH):
                halves = []
                for hf in range(2):
                    srcv = ap["w2"][e, dc][:, hf * 16:(hf + 1) * 16, :]
                    halves.append(load_slab(pack_src(srcv)))
                mmt = {}
                for tt in range(2):
                    bidx = 2 * (dc % 2) + tt
                    tok = None
                    for hf in range(2):
                        sl_2 = halves[hf]
                        for kcl in range(DCH):
                            kc = hf * DCH + kcl
                            w = [sl_2[2], tA_last[0], tA_last[1],
                                 gate.get(("B", bidx))] if kcl == 0 else ()
                            tok = MM(B[bidx], slab_tile(sl_2[1], kcl),
                                     h1_tile(kc)[:, tt * 512:(tt + 1) * 512],
                                     start=(hf == 0 and kcl == 0),
                                     stop=(hf == 1 and kcl == DCH - 1), waits=w)
                    mmt[tt] = tok
                slab_done(halves[0][0], mmt[1])
                slab_done(halves[1][0], mmt[1])
                for tt in range(2):
                    sl_ = slice(tt * 512, (tt + 1) * 512)
                    t1 = TT(c_sm[:, sl_], B[2 * (dc % 2) + tt], e_sm[:, sl_],
                            ALU.mult,
                            waits=[mmt[tt], tgb[tt],
                                   (t_acc[(dc, tt)] if e == 0 else None),
                                   gate.get(("c9", tt))])
                    gate[("B", 2 * (dc % 2) + tt)] = t1
                    t2 = TT(resid_t(dc, tt), resid_t(dc, tt), c_sm[:, sl_],
                            ALU.add)
                    gate[("c9", tt)] = t2
                tB_mm_last = mmt[1]
            t_h1_used = tB_mm_last
        t_moe_last = gate[("c9", 1)]

        p.phase = 10
        t_stats10_last = None
        for dc in range(DCH):
            for tt in range(2):
                sl_ = slice(tt * 512, (tt + 1) * 512)
                tqs = ACT(s_sm[:, sl_], resid_t(dc, tt), AF.Square,
                          waits=[t_moe_last, gate.get(("sq4", tt))])
                ts_, tqq = ln_stats(dc, tt, resid_t(dc, tt), s_sm[:, sl_],
                                    dc == 0, dc == DCH - 1,
                                    [t_moe_last, tqs] +
                                    ([gate.get(("B6r", 0)), gate.get(("B6r", 1)),
                                      gate.get(("B6r", 2)), gate.get(("B7r", 0)),
                                      gate.get("adot")] if dc == 0 else []))
                gate[("sq4", tt)] = tqq
                t_stats10_last = tqq
        t_rows10 = ln_rows([t_stats10_last])
        t_st_gate = {}
        for dc in range(DCH):
            stage = s_sm if dc % 2 == 0 else c_sm
            for tt in range(2):
                sl_ = slice(tt * 512, (tt + 1) * 512)
                t1 = TT(a_sm[:, 0:512], resid_t(dc, tt), d_sm[:, sl_],
                        ALU.subtract,
                        waits=[t_rows10] if dc == 0 and tt == 0 else ())
                t2 = TT(stage[:, sl_], a_sm[:, 0:512], e_sm[:, sl_], ALU.mult,
                        waits=[t_st_gate.get((dc % 2, tt))])
                dst = yout[tt, dc * 128:(dc + 1) * 128, :]
                t_st_gate[(dc % 2, tt)] = DMA("sync", dst, stage[:, sl_],
                                              waits=[t2])
        p.phase = 0
        fin = []
        for q, pref in (("sync", "s"), ("gpsimd", "g")):
            eps = p.qepochs[q]
            for par in (0, 1, 2):
                n = sum(eps[i][0] for i in range(len(eps)) if i % 3 == par)
                if n:
                    fin.append((Tok(pref + str(par), 16 * n, "none"), True))
        p.emit("sync", lambda sy: sy.nop(), waits=fin)

        p.replay(block, sems)

    return nc, dbg_outs



def _pack_lhsT(w):
    K, M = w.shape
    return np.ascontiguousarray(
        w.reshape(K // 128, 128, M // 128, 128).transpose(2, 1, 0, 3))


def _prep_inputs(inputs):
    f32 = np.float32
    x = np.asarray(inputs["L0_attn_hidden_states"], f32)
    xT = np.ascontiguousarray(x.transpose(0, 2, 1))

    wmix = np.stack([_pack_lhsT(np.asarray(inputs["L0_attn_Wf"], f32)),
                     _pack_lhsT(np.asarray(inputs["L0_attn_Wi"], f32)),
                     _pack_lhsT(np.asarray(inputs["L0_attn_Wg"], f32))])
    wo = _pack_lhsT(np.asarray(inputs["L0_attn_Wo"], f32))
    wqkv = _pack_lhsT(np.ascontiguousarray(
        np.asarray(inputs["L0_ffn_qkv_weight"], f32).T))
    wproj = _pack_lhsT(np.ascontiguousarray(
        np.asarray(inputs["L0_ffn_proj_weight"], f32).T))
    w1 = np.stack([_pack_lhsT(np.asarray(inputs["L0_ffn_moe_w1"], f32)[e])
                   for e in range(NE)])
    w2 = np.stack([_pack_lhsT(np.asarray(inputs["L0_ffn_moe_w2"], f32)[e])
                   for e in range(NE)])
    router = np.ascontiguousarray(
        np.asarray(inputs["L0_ffn_router_weight"], f32).reshape(DCH, 128, 8))

    def col(v):
        return np.ascontiguousarray(np.asarray(v, f32).reshape(DCH, 128).T)

    shared = {
        "wmix": wmix, "wo": wo, "wqkv": wqkv, "wproj": wproj,
        "w1": w1, "w2": w2, "router": router,
        "gnw": col(inputs["L0_attn_g_norm_weight"]),
        "l1w": col(inputs["L0_ffn_ln1_weight"]),
        "l1b": col(inputs["L0_ffn_ln1_bias"]),
        "l2w": col(inputs["L0_ffn_ln2_weight"]),
        "l2b": col(inputs["L0_ffn_ln2_bias"]),
        "eye128": np.eye(128, dtype=f32),
    }
    eye8 = np.zeros((8, 8, 128), f32)
    for e in range(8):
        eye8[e, e, :] = 1.0
    shared["eye8"] = eye8

    in_maps = []
    for c in range(NCORE):
        pm = np.zeros((128, 8), f32)
        if c > 0:
            pm[:, c - 1] = 1.0
        m = dict(shared)
        m["x0"] = np.ascontiguousarray(xT[:, :, c * TC:(c + 1) * TC])
        m["pmask"] = pm
        in_maps.append(m)
    return in_maps


def kernel(**inputs):
    in_maps = _prep_inputs(inputs)
    nc, _ = build_program()
    res = run_bass_kernel_spmd(nc, in_maps, core_ids=list(range(NCORE)))
    out = np.concatenate([res.results[c]["y"] for c in range(NCORE)], axis=2)
    return np.ascontiguousarray(out.transpose(0, 2, 1)).astype(np.float32)
